# version 01
# baseline (speedup 1.0000x reference)
import math
from contextlib import ExitStack

import numpy as np

import concourse.bass as bass
import concourse.mybir as mybir
from concourse.bass_utils import run_bass_kernel_spmd

F32 = mybir.dt.float32
BF16 = mybir.dt.bfloat16
AF = mybir.ActivationFunctionType
ALU = mybir.AluOpType
AX = mybir.AxisListType

NCORE = 8
T = 8192
TC = T // NCORE
D = 4096
NT = TC // 128
SCALE = 1.0 / math.sqrt(128.0)


class Sem:
    def __init__(self, h):
        self.h = h
        self.v = 0


class Prog:

    ENGINES = ("sync", "scalar", "vector", "tensor", "gpsimd")

    def __init__(self, name):
        self.name = name
        self.nc = bass.Bass("TRN2", target_bir_lowering=False)
        self.ops = {e: [] for e in self.ENGINES}
        self.ctx = ExitStack()
        self.store_n = 0

    def sem(self, name):
        return Sem(self.ctx.enter_context(self.nc.semaphore(name)))

    def sbuf(self, name, shape, dt):
        return self.ctx.enter_context(self.nc.sbuf_tensor(name, shape, dt))

    def psum(self, name, shape, dt=F32):
        return self.ctx.enter_context(self.nc.psum_tensor(name, shape, dt))

    def inp(self, name, shape, dt):
        return self.nc.dram_tensor(name, shape, dt, kind="ExternalInput").ap()

    def out(self, name, shape, dt):
        return self.nc.dram_tensor(name, shape, dt, kind="ExternalOutput").ap()

    def on(self, eng, fn):
        self.ops[eng].append(fn)

    def wait(self, eng, sem, val=None):
        v = sem.v if val is None else val
        if v > 0:
            self.on(eng, lambda e, s=sem.h, v=v: e.wait_ge(s, v))

    def dma(self, eng, out, in_, sem, **kw):
        sem.v += 16
        self.on(eng, lambda e, o=out, i=in_, s=sem.h, kw=kw: e.dma_start(out=o, in_=i, **kw).then_inc(s, 16))

    def run(self, eng, fn, sem=None, n=1):
        if sem is None:
            self.on(eng, fn)
        else:
            sem.v += n
            self.on(eng, lambda e, f=fn, s=sem.h, n=n: f(e).then_inc(s, n))

    def build(self):
        block = self.ctx.enter_context(self.nc.Block())
        for name in self.ENGINES:
            ops = self.ops[name]
            if not ops:
                continue

            def runner(e, ops=ops):
                for f in ops:
                    f(e)

            getattr(block, name)(runner)
        self.ctx.close()
        return self.nc



def build_l0a(upto=5):
    p = Prog("l0a")
    x = p.inp("x", [TC, D], F32)
    wq = p.inp("wq", [D, D], F32)
    wk = p.inp("wk", [D, D], F32)
    wv = p.inp("wv", [D, D], F32)
    bias_in = p.inp("biases", [1, 3 * D + 64], F32)
    qraw_d = p.nc.dram_tensor("qraw_d", [D, TC], BF16, kind="Internal").ap()
    gqt = p.inp("gqt", [128, 32], F32)
    gkt = p.inp("gkt", [128, 32], F32)
    rwt = p.inp("rwt", [D, 64], F32)
    ident = p.inp("ident", [128, 128], F32)

    qt = p.out("qt", [D, TC], BF16)
    kt = p.out("kt", [D, TC], BF16)
    vout = p.out("v", [TC, D], BF16)
    wmt = p.out("wmt", [64, TC], BF16)
    wsum = p.out("wsum", [128, NT], F32)
    meano = p.out("mean", [128, NT], F32)
    rstdo = p.out("rstd", [128, NT], F32)

    SW = 256
    NSLAB = D // SW

    x_sb = p.sbuf("x_sb", [128, D], F32)
    sq_sb = p.sbuf("sq_sb", [128, 1024], F32)
    xn_sb = p.sbuf("xn_sb", [128, D], BF16)
    xnt = p.sbuf("xnt", [128, 32, TC], BF16)
    qld = p.sbuf("qld", [128, 2, 512], BF16)
    wslab = p.sbuf("wslab", [128, 2, 32, SW], BF16)
    ident_sb = p.sbuf("ident_sb", [128, 128], BF16)
    gq_sb = p.sbuf("gq_sb", [128, 32], F32)
    gk_sb = p.sbuf("gk_sb", [128, 32], F32)
    rwt_sb = p.sbuf("rwt_sb", [128, 32, 64], BF16)
    bias_sb = p.sbuf("bias_sb", [1, D], BF16)
    rb_sb = p.sbuf("rb_sb", [1, 64], BF16)
    ones_k1 = p.sbuf("ones_k1", [1, 512], BF16)
    ones_f = p.sbuf("ones_f", [128, 1], F32)
    ones_row_f = p.sbuf("ones_row_f", [1, 128], F32)
    mean_sb = p.sbuf("mean_sb", [128, NT], F32)
    rstd_sb = p.sbuf("rstd_sb", [128, NT], F32)
    negmr_sb = p.sbuf("negmr_sb", [128, NT], F32)
    wsum_sb = p.sbuf("wsum_sb", [128, NT], F32)
    st1 = p.sbuf("st1", [128, 4], F32)
    rrow = p.sbuf("rrow", [1, 2, TC], F32)
    rstdrep = p.sbuf("rstdrep", [128, TC], F32)
    stage = p.sbuf("stage", [128, 4, 512], BF16)
    sm_sb = p.sbuf("sm_sb", [128, 64], F32)
    top8 = p.sbuf("top8", [128, 8], F32)
    msk_sb = p.sbuf("msk_sb", [128, 64], F32)
    wm_bf = p.sbuf("wm_bf", [128, 64], BF16)
    selem = p.sbuf("selem", [128, 2], F32)
    rec_sb = p.sbuf("rec_sb", [128, 1], F32)
    eps_sb = p.sbuf("eps_sb", [128, 2], F32)

    ps_tr = p.psum("ps_tr", [128, 2, 1024], BF16)
    ps_mm = p.psum("ps_mm", [128, 4, 512], F32)
    ps_ss = p.psum("ps_ss", [1, 2, 512], F32)

    s_in = p.sem("s_in")
    s_x = p.sem("s_x")
    s_w = p.sem("s_w")
    s_pe = p.sem("s_pe")
    s_act = p.sem("s_act")
    s_vec = p.sem("s_vec")
    s_st = p.sem("s_st")
    s_ld = p.sem("s_ld")

    def stage_use(eng):
        slot = p.store_n % 4
        need = 16 * (p.store_n - 3)
        if need > 0:
            p.wait(eng, s_st, need)
        p.store_n += 1
        return slot

    p.dma("gpsimd", ident_sb[:, :], ident, s_in)
    p.dma("sync", gq_sb[:, :], gqt, s_in)
    p.dma("sync", gk_sb[:, :], gkt, s_in)
    p.dma("gpsimd", rwt_sb[:, :, :], rwt.rearrange("(c p) n -> p c n", p=128), s_in)
    p.dma("gpsimd", bias_sb[:, :], bias_in[0:1, 0:D], s_in)
    p.dma("gpsimd", rb_sb[:, :], bias_in[0:1, 3 * D:3 * D + 64], s_in)
    consts_done = s_in.v
    p.run("vector", lambda e: e.memset(ones_k1[:, :], 1.0), s_vec)
    p.run("vector", lambda e: e.memset(ones_f[:, :], 1.0), s_vec)
    p.run("vector", lambda e: e.memset(ones_row_f[:, :], 1.0), s_vec)
    p.run("vector", lambda e: e.memset(eps_sb[:, 0:1], 1e-5), s_vec)
    p.run("vector", lambda e: e.memset(eps_sb[:, 1:2], 1e-6), s_vec)
    vconst_done = s_vec.v

    for t in range(NT):
        if t > 0:
            p.wait("sync", s_act)
        p.dma("sync", x_sb[:, :], x[t * 128:(t + 1) * 128, :], s_x)
        xload = s_x.v
        p.wait("vector", s_x, xload)
        p.run("vector", lambda e: e.reduce_sum(st1[:, 0:1], x_sb[:, :], axis=AX.X), s_vec)
        for i in range(4):
            p.wait("scalar", s_x, xload)
            if i > 0:
                p.wait("scalar", s_vec)
            p.run("scalar", lambda e, i=i: e.activation(sq_sb[:, :], x_sb[:, i * 1024:(i + 1) * 1024], AF.Square), s_act)
            p.wait("vector", s_act)
            if i == 0:
                p.run("vector", lambda e: e.reduce_sum(st1[:, 1:2], sq_sb[:, :], axis=AX.X), s_vec)
            else:
                p.run("vector", lambda e: e.reduce_sum(st1[:, 2:3], sq_sb[:, :], axis=AX.X), s_vec)
                p.wait("vector", s_vec)
                p.run("vector", lambda e: e.tensor_add(st1[:, 1:2], st1[:, 1:2], st1[:, 2:3]), s_vec)
            p.wait("vector", s_vec)
        p.run("vector", lambda e, t=t: e.tensor_scalar_mul(mean_sb[:, t:t + 1], st1[:, 0:1], 1.0 / D), s_vec)
        p.wait("vector", s_vec)
        p.run("vector", lambda e, t=t: e.tensor_mul(st1[:, 3:4], mean_sb[:, t:t + 1], mean_sb[:, t:t + 1]), s_vec)
        p.wait("vector", s_vec)
        p.run("vector", lambda e: e.scalar_tensor_tensor(st1[:, 2:3], st1[:, 1:2], 1.0 / D, st1[:, 3:4], ALU.mult, ALU.subtract), s_vec)
        p.wait("scalar", s_vec)
        p.run("scalar", lambda e: e.activation(st1[:, 3:4], st1[:, 2:3], AF.Ln, bias=eps_sb[:, 0:1]), s_act)
        p.wait("scalar", s_act)
        p.run("scalar", lambda e, t=t: e.activation(rstd_sb[:, t:t + 1], st1[:, 3:4], AF.Exp, scale=-0.5), s_act)
        p.wait("vector", s_act)
        p.run("vector", lambda e, t=t: e.tensor_scalar(xn_sb[:, :], x_sb[:, :], mean_sb[:, t:t + 1], rstd_sb[:, t:t + 1], ALU.subtract, ALU.mult), s_vec)
        xn_ready = s_vec.v
        p.wait("tensor", s_vec, xn_ready)
        p.wait("tensor", s_in, consts_done)
        for fc in range(32):
            pb = fc % 2
            if fc >= 2:
                p.wait("tensor", s_act)
            p.run("tensor", lambda e, fc=fc, pb=pb: e.transpose(ps_tr[:, pb, 0:128], xn_sb[:, fc * 128:(fc + 1) * 128], ident_sb[:, :]), s_pe)
            p.wait("scalar", s_pe)
            p.run("scalar", lambda e, fc=fc, pb=pb, t=t: e.copy(xnt[:, fc, t * 128:(t + 1) * 128], ps_tr[:, pb, 0:128]), s_act)

    stats_act = s_act.v

    wslab_guard = {0: 0, 1: 0}
    bias_guard = [0]
    mmfree = {}

    def wait_bank(b):
        if b in mmfree:
            av, vv = mmfree[b]
            p.wait("tensor", s_act, av)
            p.wait("tensor", s_vec, vv)

    def gemm_qk(w, bias_off, out_ap, gain_sb, first):
        bankfree = mmfree
        p.wait("gpsimd", s_pe, bias_guard[0])
        p.dma("gpsimd", bias_sb[:, :], bias_in[0:1, bias_off:bias_off + D], s_in)
        bias_ready = s_in.v
        qd_guard = s_st.v
        for s in range(NSLAB):
            sb = s % 2
            p.wait("gpsimd", s_pe, wslab_guard[sb])
            p.dma("gpsimd", wslab[:, sb, :, :], w[:, s * SW:(s + 1) * SW].rearrange("(c p) n -> p c n", p=128), s_w)
            w_ready = s_w.v
            for ds in range(SW // 128):
                c = s * (SW // 128) + ds
                for g in range(2):
                    idx = c * 2 + g
                    pbank = idx % 4
                    p.wait("tensor", s_w, w_ready)
                    if first and idx == 0:
                        p.wait("tensor", s_act, stats_act)
                        p.wait("tensor", s_vec, vconst_done)
                    if idx == 0:
                        p.wait("tensor", s_in, bias_ready)
                    if pbank in bankfree:
                        av, vv = bankfree[pbank]
                        p.wait("tensor", s_act, av)
                        p.wait("tensor", s_vec, vv)
                    for fc in range(32):
                        p.run("tensor", lambda e, fc=fc, sb=sb, ds=ds, g=g, pbank=pbank: e.matmul(
                            ps_mm[:, pbank, :], wslab[:, sb, fc, ds * 128:(ds + 1) * 128],
                            xnt[:, fc, g * 512:(g + 1) * 512],
                            start=(fc == 0), stop=False), None)
                    p.run("tensor", lambda e, pbank=pbank, c=c: e.matmul(
                        ps_mm[:, pbank, :], bias_sb[0:1, c * 128:(c + 1) * 128],
                        ones_k1[0:1, :], start=False, stop=True), s_pe, 1)
                    mm_done = s_pe.v
                    p.wait("scalar", s_pe, mm_done)
                    p.run("scalar", lambda e, pbank=pbank: e.activation(sq_sb[:, 0:512], ps_mm[:, pbank, :], AF.Square), s_act)
                    sqd = s_act.v
                    p.wait("tensor", s_act, sqd)
                    p.run("tensor", lambda e, g=g, c=c: e.matmul(ps_ss[:, g, :], ones_f[:, :], sq_sb[:, 0:512],
                                                                 start=(c == 0), stop=(c == 31)), s_pe, 1)
                    slot = stage_use("scalar")
                    p.run("scalar", lambda e, slot=slot, pbank=pbank: e.copy(stage[:, slot, :], ps_mm[:, pbank, :]), s_act)
                    bankfree[pbank] = (s_act.v, 0)
                    p.wait("sync", s_act)
                    p.dma("sync", qraw_d.rearrange("(c p) n -> p c n", p=128)[:, c, g * 512:(g + 1) * 512],
                          stage[:, slot, :], s_st)
            wslab_guard[sb] = s_pe.v
        bias_guard[0] = s_pe.v
        ss_done = s_pe.v
        spill_done = s_st.v
        p.wait("scalar", s_pe, ss_done)
        p.run("scalar", lambda e: e.activation(rrow[0:1, 0, :], ps_ss[0:1, :, :].rearrange("p a b -> p (a b)"), AF.Ln, scale=1.0 / D, bias=eps_sb[0:1, 1:2]), s_act)
        p.wait("scalar", s_act)
        p.run("scalar", lambda e: e.activation(rrow[0:1, 1, :], rrow[0:1, 0, :], AF.Exp, scale=-0.5), s_act)
        rr_done = s_act.v
        for g in range(2):
            p.wait("tensor", s_act, rr_done)
            p.wait("tensor", s_vec, vconst_done)
            wait_bank(3)
            if g == 1:
                p.wait("tensor", s_vec)
            p.run("tensor", lambda e, g=g: e.matmul(ps_mm[:, 3, :], ones_row_f[0:1, :], rrow[0:1, 1, g * 512:(g + 1) * 512], start=True, stop=True), s_pe, 1)
            p.wait("vector", s_pe)
            p.run("vector", lambda e, g=g: e.tensor_copy(rstdrep[:, g * 512:(g + 1) * 512], ps_mm[:, 3, :]), s_vec)
        bankfree[3] = (s_act.v, s_vec.v)
        rep_done = s_vec.v
        ld_guard = {}
        for c in range(32):
            for g in range(2):
                idx = c * 2 + g
                lb = idx % 2
                if idx == 0:
                    p.wait("sync", s_st, spill_done)
                if lb in ld_guard:
                    p.wait("sync", s_vec, ld_guard[lb])
                p.dma("sync", qld[:, lb, :], qraw_d.rearrange("(c p) n -> p c n", p=128)[:, c, g * 512:(g + 1) * 512], s_ld)
                ld_v = s_ld.v
                slot = stage_use("vector")
                p.wait("vector", s_ld, ld_v)
                if idx == 0:
                    p.wait("vector", s_vec, rep_done)
                p.run("vector", lambda e, c=c, g=g, slot=slot, lb=lb, gain_sb=gain_sb: e.scalar_tensor_tensor(
                    stage[:, slot, :], qld[:, lb, :], gain_sb[:, c:c + 1],
                    rstdrep[:, g * 512:(g + 1) * 512], ALU.mult, ALU.mult), s_vec)
                ld_guard[lb] = s_vec.v
                p.wait("sync", s_vec)
                p.dma("sync", out_ap.rearrange("(c p) n -> p c n", p=128)[:, c, g * 512:(g + 1) * 512],
                      stage[:, slot, :], s_st)

    try:
        if upto >= 2:
            gemm_qk(wq, 0, qt, gq_sb, True)
        if upto >= 3:
            gemm_qk(wk, D, kt, gk_sb, False)

        if upto < 4:
            raise StopIteration
        p.wait("gpsimd", s_pe, bias_guard[0])
        p.dma("gpsimd", bias_sb[:, :], bias_in[0:1, 2 * D:3 * D], s_in)
        bv_ready = s_in.v
        for s in range(NSLAB):
            sb = s % 2
            p.wait("gpsimd", s_pe, wslab_guard[sb])
            p.dma("gpsimd", wslab[:, sb, :, :], wv[:, s * SW:(s + 1) * SW].rearrange("(c p) n -> p c n", p=128), s_w)
            w_ready = s_w.v
            for t in range(NT):
                pbank = t % 4
                p.wait("tensor", s_w, w_ready)
                wait_bank(pbank)
                for fc in range(32):
                    p.run("tensor", lambda e, fc=fc, sb=sb, t=t, pbank=pbank: e.matmul(
                        ps_mm[:, pbank, 0:SW], xnt[:, fc, t * 128:(t + 1) * 128],
                        wslab[:, sb, fc, :], start=(fc == 0), stop=False), None)
                if s == 0 and t == 0:
                    p.wait("tensor", s_in, bv_ready)
                p.run("tensor", lambda e, pbank=pbank, s=s: e.matmul(
                    ps_mm[:, pbank, 0:SW], ones_k1[0:1, 0:128], bias_sb[0:1, s * SW:(s + 1) * SW],
                    start=False, stop=True), s_pe, 1)
                mm_done = s_pe.v
                p.wait("scalar", s_pe, mm_done)
                slot = stage_use("scalar")
                p.run("scalar", lambda e, slot=slot, pbank=pbank: e.copy(stage[:, slot, 0:SW], ps_mm[:, pbank, 0:SW]), s_act)
                mmfree[pbank] = (s_act.v, 0)
                p.wait("sync", s_act)
                p.dma("sync", vout[t * 128:(t + 1) * 128, s * SW:(s + 1) * SW], stage[:, slot, 0:SW], s_st)
            wslab_guard[sb] = s_pe.v

        if upto < 5:
            raise StopIteration
        wait_bank(0)
        for t in range(NT):
            if t > 0:
                p.wait("tensor", s_act)
                p.wait("tensor", s_vec)
            for fc in range(32):
                p.run("tensor", lambda e, fc=fc, t=t: e.matmul(ps_mm[:, 0, 0:64], xnt[:, fc, t * 128:(t + 1) * 128],
                                                               rwt_sb[:, fc, :], start=(fc == 0), stop=False), None)
            p.run("tensor", lambda e: e.matmul(ps_mm[:, 0, 0:64], ones_k1[0:1, 0:128], rb_sb[0:1, :],
                                               start=False, stop=True), s_pe, 1)
            lg = s_pe.v
            p.wait("vector", s_pe, lg)
            p.run("vector", lambda e: e.reduce_max(selem[:, 0:1], ps_mm[:, 0, 0:64], axis=AX.X, negate=True), s_vec)
            p.wait("scalar", s_vec)
            p.run("scalar", lambda e: e.activation(sm_sb[:, :], ps_mm[:, 0, 0:64], AF.Exp, bias=selem[:, 0:1], accum_out=selem[:, 1:2]), s_act)
            p.wait("vector", s_act)
            p.run("vector", lambda e: e.reciprocal(rec_sb[:, :], selem[:, 1:2]), s_vec)
            p.wait("vector", s_vec)
            p.run("vector", lambda e: e.tensor_scalar_mul(sm_sb[:, :], sm_sb[:, :], rec_sb[:, 0:1]), s_vec)
            p.wait("vector", s_vec)
            p.run("vector", lambda e: e.max(out=top8[:, :], in_=sm_sb[:, :]), s_vec)
            p.wait("vector", s_vec)
            p.run("vector", lambda e, t=t: e.reduce_sum(wsum_sb[:, t:t + 1], top8[:, :], axis=AX.X), s_vec)
            p.run("vector", lambda e: e.tensor_single_scalar(msk_sb[:, :], sm_sb[:, :], top8[:, 7:8], ALU.is_ge), s_vec)
            p.wait("vector", s_vec)
            p.run("vector", lambda e: e.tensor_mul(wm_bf[:, :], sm_sb[:, :], msk_sb[:, :]), s_vec)
            wm_done = s_vec.v
            p.wait("tensor", s_vec, wm_done)
            if t > 0:
                p.wait("tensor", s_act)
            p.run("tensor", lambda e: e.transpose(ps_tr[0:64, 0, 0:128], wm_bf[:, :], ident_sb[:, :]), s_pe, 1)
            p.wait("scalar", s_pe)
            slot = stage_use("scalar")
            p.run("scalar", lambda e, slot=slot: e.copy(stage[0:64, slot, 0:128], ps_tr[0:64, 0, 0:128]), s_act)
            p.wait("sync", s_act)
            p.dma("sync", wmt[:, t * 128:(t + 1) * 128], stage[0:64, slot, 0:128], s_st)

    except StopIteration:
        pass

    p.wait("sync", s_vec)
    p.dma("sync", wsum, wsum_sb[:, :], s_st)
    p.dma("sync", meano, mean_sb[:, :], s_st)
    p.dma("sync", rstdo, rstd_sb[:, :], s_st)
    p.wait("sync", s_st)
    return p.build()



_NC_CACHE = {}


def _bf16np():
    import ml_dtypes
    return ml_dtypes.bfloat16


def _get(name, builder):
    if name not in _NC_CACHE:
        _NC_CACHE[name] = builder()
    return _NC_CACHE[name]


def run_l0a(inputs):
    x = np.asarray(inputs["L0_attn_hidden_states"], np.float32).reshape(T, D)
    biases = np.concatenate([
        np.asarray(inputs["L0_attn_bq"], np.float32),
        np.asarray(inputs["L0_attn_bk"], np.float32),
        np.asarray(inputs["L0_attn_bv"], np.float32),
        np.asarray(inputs["L0_ffn_router_bias"], np.float32),
    ])[None, :]
    common = dict(
        wq=np.asarray(inputs["L0_attn_Wq"], np.float32),
        wk=np.asarray(inputs["L0_attn_Wk"], np.float32),
        wv=np.asarray(inputs["L0_attn_Wv"], np.float32),
        biases=biases,
        gqt=np.ascontiguousarray(np.asarray(inputs["L0_attn_gq"], np.float32).reshape(32, 128).T),
        gkt=np.ascontiguousarray(np.asarray(inputs["L0_attn_gk"], np.float32).reshape(32, 128).T),
        rwt=np.ascontiguousarray(np.asarray(inputs["L0_ffn_router_weight"], np.float32).T),
        ident=np.eye(128, dtype=np.float32),
    )
    import os
    upto = int(os.environ.get("L0A_UPTO", "5"))
    nc = _get(f"l0a{upto}", lambda: build_l0a(upto))
    in_maps = [dict(common, x=np.ascontiguousarray(x[c * TC:(c + 1) * TC])) for c in range(NCORE)]
    res = run_bass_kernel_spmd(nc, in_maps, core_ids=list(range(NCORE)))
    return res.results



def build_l0c(nh=32):
    p = Prog("l0c")
    qt = p.inp("qt", [D, TC], BF16)
    ktf = p.inp("ktf", [D, T], BF16)
    vf = p.inp("vf", [T, D], BF16)
    identb = p.inp("identb", [128, 128], BF16)
    ot = p.out("ot", [D, TC], BF16)

    kbuf = p.sbuf("kbuf", [128, 2, T], BF16)
    vbuf = p.sbuf("vbuf", [128, 2, 64, 130], BF16)
    qbuf = p.sbuf("qbuf", [128, 2, TC], BF16)
    pt = p.sbuf("pt", [128, 2, 512], BF16)
    o_sb = p.sbuf("o_sb", [128, 2, 128], BF16)
    stage = p.sbuf("stage", [128, 4, 128], BF16)
    rec = p.sbuf("rec", [128, 4], F32)
    lcol = p.sbuf("lcol", [128, 4], F32)

    ps_s = p.psum("ps_s", [128, 2, 512], F32)
    ps_o = p.psum("ps_o", [128, 4, 512], F32)
    ps_t = p.psum("ps_t", [128, 2, 1024], BF16)

    ident_sb = p.sbuf("ident_sb_c", [128, 128], BF16)
    s_id = p.sem("s_id")
    s_k = p.sem("s_k")
    s_v = p.sem("s_v")
    s_q = p.sem("s_q")
    s_pe = p.sem("s_pe")
    s_pv = p.sem("s_pv")
    s_act = p.sem("s_act")
    s_vec = p.sem("s_vec")
    s_st = p.sem("s_st")

    p.dma("gpsimd", ident_sb[:, :], identb, s_id)
    id_done = s_id.v
    p.run("vector", lambda e: e.memset(vbuf[:, 0, :, 128:129], 1.0), s_vec)
    p.run("vector", lambda e: e.memset(vbuf[:, 1, :, 128:129], 1.0), s_vec)
    ones_done = s_vec.v

    def issue_loads(h, guard_pe):
        b = h % 2
        if guard_pe > 0:
            p.wait("sync", s_pv, guard_pe)
        p.dma("sync", kbuf[:, b, :], ktf[h * 128:(h + 1) * 128, :], s_k)
        for q in range(4):
            p.dma("sync", vbuf[:, b, q * 16:(q + 1) * 16, 0:128],
                  vf.rearrange("(kc p) n -> p kc n", p=128)[:, q * 16:(q + 1) * 16, h * 128:(h + 1) * 128], s_v)
        p.dma("sync", qbuf[:, b, :], qt[h * 128:(h + 1) * 128, :], s_q)
        return s_k.v, s_v.v, s_q.v

    buf_guard = {0: 0, 1: 0}
    loads = {}
    loads[0] = issue_loads(0, 0)
    store_n = 0

    for h in range(nh):
        b = h % 2
        if h + 1 < nh:
            loads[h + 1] = issue_loads(h + 1, buf_guard[(h + 1) % 2])
        kv_, vv_, qv_ = loads[h]
        exp_hist = []
        pv_hist = []
        for qg in range(2):
            evac_guard = s_vec.v
            for kc in range(64):
                sbank = kc % 2
                if kc == 0:
                    p.wait("tensor", s_k, kv_)
                    p.wait("tensor", s_q, qv_)
                    if qg == 0 and h == 0:
                        p.wait("tensor", s_vec, ones_done)
                    p.wait("tensor", s_vec, evac_guard)
                if len(exp_hist) >= 2:
                    p.wait("tensor", s_act, exp_hist[-2])
                p.run("tensor", lambda e, b=b, kc=kc, qg=qg, sbank=sbank: e.matmul(
                    ps_s[:, sbank, :], kbuf[:, b, kc * 128:(kc + 1) * 128],
                    qbuf[:, b, qg * 512:(qg + 1) * 512], start=True, stop=True), s_pe, 1)
                qk_v = s_pe.v
                p.wait("scalar", s_pe, qk_v)
                if len(pv_hist) >= 2:
                    p.wait("scalar", s_pv, pv_hist[-2])
                p.run("scalar", lambda e, sbank=sbank: e.activation(
                    pt[:, sbank, :], ps_s[:, sbank, :], AF.Exp, scale=SCALE), s_act)
                exp_hist.append(s_act.v)
                p.wait("tensor", s_act, exp_hist[-1])
                if kc == 0:
                    p.wait("tensor", s_v, vv_)
                for j in range(4):
                    p.run("tensor", lambda e, b=b, kc=kc, j=j, sbank=sbank: e.matmul(
                        ps_o[:, j, 0:129], pt[:, sbank, j * 128:(j + 1) * 128],
                        vbuf[:, b, kc, 0:129], start=(kc == 0), stop=(kc == 63)),
                        s_pv if j == 3 else None, 1)
                pv_hist.append(s_pv.v)
            pv_done = s_pv.v
            for j in range(4):
                jb = j % 2
                p.wait("vector", s_pv, pv_done)
                if j >= 2:
                    p.wait("vector", s_pe)
                p.run("vector", lambda e, j=j: e.tensor_copy(lcol[:, j:j + 1], ps_o[:, j, 128:129]), s_vec)
                p.wait("vector", s_vec)
                p.run("vector", lambda e, j=j: e.reciprocal(rec[:, j:j + 1], lcol[:, j:j + 1]), s_vec)
                p.wait("vector", s_vec)
                p.run("vector", lambda e, j=j, jb=jb: e.tensor_scalar_mul(
                    o_sb[:, jb, :], ps_o[:, j, 0:128], rec[:, j:j + 1]), s_vec)
                p.wait("tensor", s_vec)
                if j >= 2:
                    p.wait("tensor", s_act)
                if h == 0 and qg == 0 and j == 0:
                    p.wait("tensor", s_id, id_done)
                p.run("tensor", lambda e, jb=jb: e.transpose(ps_t[:, jb, 0:128], o_sb[:, jb, :], ident_sb[:, :]), s_pe, 1)
                p.wait("scalar", s_pe)
                slot = store_n % 4
                need = 16 * (store_n - 3)
                if need > 0:
                    p.wait("scalar", s_st, need)
                store_n += 1
                p.run("scalar", lambda e, slot=slot, jb=jb: e.copy(stage[:, slot, :], ps_t[:, jb, 0:128]), s_act)
                p.wait("sync", s_act)
                p.dma("sync", ot[h * 128:(h + 1) * 128, qg * 512 + j * 128:qg * 512 + (j + 1) * 128],
                      stage[:, slot, :], s_st)
        buf_guard[b] = s_pv.v
    p.wait("sync", s_st)
    return p.build()


def run_l0c(l0a_res):
    ktf = np.concatenate([l0a_res[c]["kt"] for c in range(NCORE)], axis=1)
    vf = np.concatenate([l0a_res[c]["v"] for c in range(NCORE)], axis=0)
    import os
    nh = int(os.environ.get("L0C_NH", "32"))
    nc = _get(f"l0c{nh}", lambda: build_l0c(nh))
    identb = np.eye(128, dtype=np.float32).astype(_bf16np())
    in_maps = [dict(qt=l0a_res[c]["qt"], ktf=ktf, vf=vf, identb=identb) for c in range(NCORE)]
    res = run_bass_kernel_spmd(nc, in_maps, core_ids=list(range(NCORE)))
    return res.results



def build_l0d_l1a():
    p = Prog("l0d_l1a")
    ot = p.inp("ot", [D, TC], BF16)
    wo = p.inp("wo", [D, D], F32)
    bo = p.inp("bo", [1, D], F32)
    eb = p.inp("eb", [64, D], F32)
    wmt_i = p.inp("wmt", [64, TC], BF16)
    wsum_i = p.inp("wsum", [128, NT], F32)
    mean_i = p.inp("mean", [128, NT], F32)
    rstd_i = p.inp("rstd", [128, NT], F32)
    x = p.inp("x", [TC, D], F32)
    wq1 = p.inp("wq1", [D, D], F32)
    wk1 = p.inp("wk1", [D, 1024], F32)
    wv1 = p.inp("wv1", [D, 1024], F32)
    gwt = p.inp("gwt", [D, 8], F32)
    ics = p.inp("ics", [8, D], F32)
    ident = p.inp("ident", [128, 128], F32)

    x1 = p.out("x1", [TC, D], F32)
    x1_d = p.nc.dram_tensor("x1_d", [TC, D], F32, kind="Internal").ap()
    qt1 = p.out("qt1", [D, TC], BF16)
    kt1 = p.out("kt1", [1024, TC], BF16)
    v1 = p.out("v1", [TC, 1024], BF16)
    rtt = p.out("rtt", [8, TC], BF16)
    xst = p.out("xst", [D, TC], BF16)

    SW = 256
    NSLAB = D // SW

    otld = p.sbuf("otld", [128, 2, 32, 128], BF16)
    wslab = p.sbuf("wslab", [128, 2, 32, SW], BF16)
    eb_sb = p.sbuf("eb_sb", [64, D], BF16)
    wmt_sb = p.sbuf("wmt_sb", [64, TC], BF16)
    wsum_sb = p.sbuf("wsum_sb", [128, NT], F32)
    mean_sb = p.sbuf("mean_sb", [128, NT], F32)
    rstd_sb = p.sbuf("rstd_sb", [128, NT], F32)
    bo_sb = p.sbuf("bo_sb", [1, D], BF16)
    xp = p.sbuf("xp", [128, 2, SW], F32)
    xnf = p.sbuf("xnf", [128, SW], F32)
    fsb = p.sbuf("fsb", [128, SW], F32)
    gsb = p.sbuf("gsb", [128, SW], F32)
    prsb = p.sbuf("prsb", [128, SW], F32)
    x1sb = p.sbuf("x1sb", [128, 2, SW], F32)
    ident_sb = p.sbuf("ident_sb", [128, 128], BF16)
    ones_k1 = p.sbuf("ones_k1", [1, 512], BF16)
    x1t_sb = p.sbuf("x1t_sb", [128, D], F32)
    sq_sb = p.sbuf("sq_sb", [128, 1024], F32)
    xn1_sb = p.sbuf("xn1_sb", [128, D], BF16)
    xs_sb = p.sbuf("xs_sb", [128, D], BF16)
    xnt1 = p.sbuf("xnt1", [128, 32, TC], BF16)
    gwt_sb = p.sbuf("gwt_sb", [128, 32, 8], BF16)
    ics_sb = p.sbuf("ics_sb", [8, D], BF16)
    st1 = p.sbuf("st1", [128, 4], F32)
    mean1_sb = p.sbuf("mean1_sb", [128, NT], F32)
    rstd1_sb = p.sbuf("rstd1_sb", [128, NT], F32)
    rt_f = p.sbuf("rt_f", [128, 8], F32)
    rt_bf = p.sbuf("rt_bf", [128, 8], BF16)
    rtT_sb = p.sbuf("rtT_sb", [8, TC], BF16)
    selem = p.sbuf("selem", [128, 2], F32)
    rec_sb = p.sbuf("rec_sb", [128, 1], F32)
    eps_sb = p.sbuf("eps_sb", [128, 1], F32)
    stage = p.sbuf("stage", [128, 4, 512], BF16)

    ps_mm = p.psum("ps_mm", [128, 4, 512], F32)
    ps_b = p.psum("ps_b", [128, 2, 512], F32)
    ps_tr = p.psum("ps_tr", [128, 2, 1024], BF16)

    s_in = p.sem("s_in")
    s_x = p.sem("s_x")
    s_w = p.sem("s_w")
    s_ot = p.sem("s_ot")
    s_pe = p.sem("s_pe")
    s_act = p.sem("s_act")
    s_vec = p.sem("s_vec")
    s_st = p.sem("s_st")

    def stage_use(eng):
        slot = p.store_n % 4
        need = 16 * (p.store_n - 3)
        if need > 0:
            p.wait(eng, s_st, need)
        p.store_n += 1
        return slot

    p.dma("gpsimd", ident_sb[:, :], ident, s_in)
    p.dma("gpsimd", eb_sb[:, :], eb, s_in)
    p.dma("gpsimd", bo_sb[:, :], bo, s_in)
    p.dma("gpsimd", gwt_sb[:, :, :], gwt.rearrange("(c p) n -> p c n", p=128), s_in)
    p.dma("gpsimd", ics_sb[:, :], ics, s_in)
    p.dma("sync", wmt_sb[:, :], wmt_i, s_in)
    p.dma("sync", wsum_sb[:, :], wsum_i, s_in)
    p.dma("sync", mean_sb[:, :], mean_i, s_in)
    p.dma("sync", rstd_sb[:, :], rstd_i, s_in)
    consts = s_in.v
    p.run("vector", lambda e: e.memset(ones_k1[:, :], 1.0), s_vec)
    p.run("vector", lambda e: e.memset(eps_sb[:, :], 1e-5), s_vec)
    vconst = s_vec.v

    mmfree = {}
    bfree = {}

    def wait_bank(b, eng="tensor"):
        if b in mmfree:
            p.wait(eng, s_vec, mmfree[b])

    wslab_guard = {0: 0, 1: 0}
    ot_guard = {0: 0, 1: 0}
    x1_dmas = {}
    first_piece = True
    for sl in range(NSLAB):
        sb = sl % 2
        p.wait("gpsimd", s_pe, wslab_guard[sb])
        p.dma("gpsimd", wslab[:, sb, :, :], wo[:, sl * SW:(sl + 1) * SW].rearrange("(c p) n -> p c n", p=128), s_w)
        w_ready = s_w.v
        for t in range(NT):
            ob = t % 2
            p.wait("sync", s_pe, ot_guard[ob])
            p.dma("sync", otld[:, ob, :, :], ot.rearrange("(c p) n -> p c n", p=128)[:, :, t * 128:(t + 1) * 128], s_ot)
            ot_ready = s_ot.v
            xb = t % 2
            p.dma("sync", xp[:, xb, :], x[t * 128:(t + 1) * 128, sl * SW:(sl + 1) * SW], s_x)
            x_ready = s_x.v
            pbank = t % 2
            p.wait("tensor", s_w, w_ready)
            p.wait("tensor", s_ot, ot_ready)
            if first_piece:
                p.wait("tensor", s_in, consts)
                p.wait("tensor", s_vec, vconst)
                first_piece = False
            wait_bank(pbank)
            for fc in range(32):
                p.run("tensor", lambda e, fc=fc, sb=sb, ob=ob, pbank=pbank: e.matmul(
                    ps_mm[:, pbank, 0:SW], otld[:, ob, fc, :], wslab[:, sb, fc, :],
                    start=(fc == 0), stop=False), None)
            p.run("tensor", lambda e, pbank=pbank, sl=sl: e.matmul(
                ps_mm[:, pbank, 0:SW], ones_k1[0:1, 0:128], bo_sb[0:1, sl * SW:(sl + 1) * SW],
                start=False, stop=True), s_pe, 1)
            s_done = s_pe.v
            ot_guard[ob] = s_pe.v
            bbank = t % 2
            if bbank in bfree:
                p.wait("tensor", s_vec, bfree[bbank])
            p.run("tensor", lambda e, bbank=bbank, t=t, sl=sl: e.matmul(
                ps_b[:, bbank, 0:SW], wmt_sb[:, t * 128:(t + 1) * 128], eb_sb[:, sl * SW:(sl + 1) * SW],
                start=True, stop=True), s_pe, 1)
            b_done = s_pe.v
            p.wait("vector", s_x, x_ready)
            p.run("vector", lambda e, xb=xb, t=t: e.tensor_scalar(
                xnf[:, :], xp[:, xb, :], mean_sb[:, t:t + 1], rstd_sb[:, t:t + 1], ALU.subtract, ALU.mult), s_vec)
            p.wait("vector", s_vec)
            p.wait("vector", s_pe, b_done)
            p.run("vector", lambda e, t=t, bbank=bbank: e.scalar_tensor_tensor(
                fsb[:, :], xnf[:, :], wsum_sb[:, t:t + 1], ps_b[:, bbank, 0:SW], ALU.mult, ALU.add), s_vec)
            bfree[bbank] = s_vec.v
            p.wait("scalar", s_vec)
            p.run("scalar", lambda e: e.activation(gsb[:, :], fsb[:, :], AF.Silu), s_act)
            p.wait("vector", s_act)
            p.run("vector", lambda e, pbank=pbank: e.tensor_mul(prsb[:, :], gsb[:, :], ps_mm[:, pbank, 0:SW]), s_vec)
            mmfree[pbank] = s_vec.v
            p.wait("vector", s_vec)
            need = 16 * (x1_dmas[xb] if xb in x1_dmas else 0)
            if need > 0:
                p.wait("vector", s_st, need)
            p.run("vector", lambda e, xb=xb: e.tensor_add(x1sb[:, xb, :], prsb[:, :], xp[:, xb, :]), s_vec)
            p.wait("sync", s_vec)
            p.dma("sync", x1[t * 128:(t + 1) * 128, sl * SW:(sl + 1) * SW], x1sb[:, xb, :], s_st)
            p.dma("sync", x1_d[t * 128:(t + 1) * 128, sl * SW:(sl + 1) * SW], x1sb[:, xb, :], s_st)
            x1_dmas[xb] = s_st.v // 16
        wslab_guard[sb] = s_pe.v
    x1_done = s_st.v

    for t in range(NT):
        if t > 0:
            p.wait("sync", s_act)
            p.wait("sync", s_vec)
        if t == 0:
            p.wait("sync", s_st, x1_done)
        p.dma("sync", x1t_sb[:, :], x1_d[t * 128:(t + 1) * 128, :], s_x)
        xload = s_x.v
        p.wait("vector", s_x, xload)
        p.run("vector", lambda e: e.reduce_sum(st1[:, 0:1], x1t_sb[:, :], axis=AX.X), s_vec)
        for i in range(4):
            p.wait("scalar", s_x, xload)
            if i > 0:
                p.wait("scalar", s_vec)
            p.run("scalar", lambda e, i=i: e.activation(sq_sb[:, :], x1t_sb[:, i * 1024:(i + 1) * 1024], AF.Square), s_act)
            p.wait("vector", s_act)
            if i == 0:
                p.run("vector", lambda e: e.reduce_sum(st1[:, 1:2], sq_sb[:, :], axis=AX.X), s_vec)
            else:
                p.run("vector", lambda e: e.reduce_sum(st1[:, 2:3], sq_sb[:, :], axis=AX.X), s_vec)
                p.wait("vector", s_vec)
                p.run("vector", lambda e: e.tensor_add(st1[:, 1:2], st1[:, 1:2], st1[:, 2:3]), s_vec)
            p.wait("vector", s_vec)
        p.run("vector", lambda e, t=t: e.tensor_scalar_mul(mean1_sb[:, t:t + 1], st1[:, 0:1], 1.0 / D), s_vec)
        p.wait("vector", s_vec)
        p.run("vector", lambda e, t=t: e.tensor_mul(st1[:, 3:4], mean1_sb[:, t:t + 1], mean1_sb[:, t:t + 1]), s_vec)
        p.wait("vector", s_vec)
        p.run("vector", lambda e: e.scalar_tensor_tensor(st1[:, 2:3], st1[:, 1:2], 1.0 / D, st1[:, 3:4], ALU.mult, ALU.subtract), s_vec)
        p.wait("scalar", s_vec)
        p.run("scalar", lambda e: e.activation(st1[:, 3:4], st1[:, 2:3], AF.Ln, bias=eps_sb[:, 0:1]), s_act)
        p.wait("scalar", s_act)
        p.run("scalar", lambda e, t=t: e.activation(rstd1_sb[:, t:t + 1], st1[:, 3:4], AF.Exp, scale=-0.5), s_act)
        p.wait("vector", s_act)
        p.run("vector", lambda e, t=t: e.tensor_scalar(xn1_sb[:, :], x1t_sb[:, :], mean1_sb[:, t:t + 1], rstd1_sb[:, t:t + 1], ALU.subtract, ALU.mult), s_vec)
        xn_ready = s_vec.v
        p.wait("tensor", s_vec, xn_ready)
        for fc in range(32):
            pb = fc % 2
            if fc >= 2:
                p.wait("tensor", s_act)
            p.run("tensor", lambda e, fc=fc, pb=pb: e.transpose(ps_tr[:, pb, 0:128], xn1_sb[:, fc * 128:(fc + 1) * 128], ident_sb[:, :]), s_pe)
            p.wait("scalar", s_pe)
            p.run("scalar", lambda e, fc=fc, pb=pb, t=t: e.copy(xnt1[:, fc, t * 128:(t + 1) * 128], ps_tr[:, pb, 0:128]), s_act)
        tr_done = s_act.v
        p.wait("tensor", s_act, tr_done)
        wait_bank(0)
        for fc in range(32):
            p.run("tensor", lambda e, fc=fc, t=t: e.matmul(ps_mm[:, 0, 0:8], xnt1[:, fc, t * 128:(t + 1) * 128],
                                                           gwt_sb[:, fc, :], start=(fc == 0), stop=(fc == 31)),
                  s_pe if fc == 31 else None, 1)
        lg = s_pe.v
        p.wait("vector", s_pe, lg)
        p.run("vector", lambda e: e.reduce_max(selem[:, 0:1], ps_mm[:, 0, 0:8], axis=AX.X, negate=True), s_vec)
        p.wait("scalar", s_vec)
        p.run("scalar", lambda e: e.activation(rt_f[:, :], ps_mm[:, 0, 0:8], AF.Exp, bias=selem[:, 0:1], accum_out=selem[:, 1:2]), s_act)
        p.wait("vector", s_act)
        p.run("vector", lambda e: e.reciprocal(rec_sb[:, :], selem[:, 1:2]), s_vec)
        p.wait("vector", s_vec)
        p.run("vector", lambda e: e.tensor_scalar_mul(rt_bf[:, :], rt_f[:, :], rec_sb[:, 0:1]), s_vec)
        mmfree[0] = s_vec.v
        rt_done = s_vec.v
        p.wait("tensor", s_vec, rt_done)
        p.wait("tensor", s_act)
        p.run("tensor", lambda e: e.transpose(ps_tr[0:8, 0, 0:128], rt_bf[:, :], ident_sb[:, :]), s_pe, 1)
        p.wait("scalar", s_pe)
        p.run("scalar", lambda e, t=t: e.copy(rtT_sb[:, t * 128:(t + 1) * 128], ps_tr[0:8, 0, 0:128]), s_act)
        rtT_done = s_act.v
        for sl2 in range(8):
            p.wait("tensor", s_act, rtT_done)
            bbank = sl2 % 2
            if bbank in bfree:
                p.wait("tensor", s_vec, bfree[bbank])
            p.run("tensor", lambda e, sl2=sl2, t=t, bbank=bbank: e.matmul(
                ps_b[:, bbank, :], rtT_sb[:, t * 128:(t + 1) * 128], ics_sb[:, sl2 * 512:(sl2 + 1) * 512],
                start=True, stop=True), s_pe, 1)
            isd = s_pe.v
            p.wait("vector", s_pe, isd)
            p.run("vector", lambda e, sl2=sl2, bbank=bbank: e.tensor_mul(
                xs_sb[:, sl2 * 512:(sl2 + 1) * 512], xn1_sb[:, sl2 * 512:(sl2 + 1) * 512], ps_b[:, bbank, :]), s_vec)
            bfree[bbank] = s_vec.v
        xs_done = s_vec.v
        p.wait("tensor", s_vec, xs_done)
        for fc in range(32):
            pb = fc % 2
            p.wait("tensor", s_act)
            p.run("tensor", lambda e, fc=fc, pb=pb: e.transpose(ps_tr[:, pb, 0:128], xs_sb[:, fc * 128:(fc + 1) * 128], ident_sb[:, :]), s_pe)
            p.wait("scalar", s_pe)
            slot = stage_use("scalar")
            p.run("scalar", lambda e, slot=slot, pb=pb: e.copy(stage[:, slot, 0:128], ps_tr[:, pb, 0:128]), s_act)
            p.wait("sync", s_act)
            p.dma("sync", xst[fc * 128:(fc + 1) * 128, t * 128:(t + 1) * 128], stage[:, slot, 0:128], s_st)
    p.wait("sync", s_act)
    p.dma("sync", rtt, rtT_sb[:, :], s_st)

    def gemm_plain(w, out_ap, ncol, featmajor):
        nslab = ncol // SW
        for sl in range(nslab):
            sb = sl % 2
            p.wait("gpsimd", s_pe, wslab_guard[sb])
            p.dma("gpsimd", wslab[:, sb, :, :], w[:, sl * SW:(sl + 1) * SW].rearrange("(c p) n -> p c n", p=128), s_w)
            w_ready = s_w.v
            if featmajor:
                for ds in range(SW // 128):
                    c = sl * (SW // 128) + ds
                    for g in range(2):
                        pbank = (c * 2 + g) % 4
                        p.wait("tensor", s_w, w_ready)
                        wait_bank(pbank)
                        for fc in range(32):
                            p.run("tensor", lambda e, fc=fc, sb=sb, ds=ds, g=g, pbank=pbank: e.matmul(
                                ps_mm[:, pbank, :], wslab[:, sb, fc, ds * 128:(ds + 1) * 128],
                                xnt1[:, fc, g * 512:(g + 1) * 512],
                                start=(fc == 0), stop=(fc == 31)),
                                  s_pe if fc == 31 else None, 1)
                        mm = s_pe.v
                        p.wait("vector", s_pe, mm)
                        slot = stage_use("vector")
                        p.run("vector", lambda e, slot=slot, pbank=pbank: e.tensor_copy(stage[:, slot, :], ps_mm[:, pbank, :]), s_vec)
                        mmfree[pbank] = s_vec.v
                        p.wait("sync", s_vec)
                        p.dma("sync", out_ap.rearrange("(c p) n -> p c n", p=128)[:, c, g * 512:(g + 1) * 512],
                              stage[:, slot, :], s_st)
            else:
                for t in range(NT):
                    pbank = t % 4
                    p.wait("tensor", s_w, w_ready)
                    wait_bank(pbank)
                    for fc in range(32):
                        p.run("tensor", lambda e, fc=fc, sb=sb, t=t, pbank=pbank: e.matmul(
                            ps_mm[:, pbank, 0:SW], xnt1[:, fc, t * 128:(t + 1) * 128],
                            wslab[:, sb, fc, :], start=(fc == 0), stop=(fc == 31)),
                              s_pe if fc == 31 else None, 1)
                    mm = s_pe.v
                    p.wait("vector", s_pe, mm)
                    slot = stage_use("vector")
                    p.run("vector", lambda e, slot=slot, pbank=pbank: e.tensor_copy(stage[:, slot, 0:SW], ps_mm[:, pbank, 0:SW]), s_vec)
                    mmfree[pbank] = s_vec.v
                    p.wait("sync", s_vec)
                    p.dma("sync", out_ap[t * 128:(t + 1) * 128, sl * SW:(sl + 1) * SW], stage[:, slot, 0:SW], s_st)
            wslab_guard[sb] = s_pe.v

    gemm_plain(wq1, qt1, D, True)
    gemm_plain(wk1, kt1, 1024, True)
    gemm_plain(wv1, v1, 1024, False)

    p.wait("sync", s_st)
    return p.build()


def run_l0d_l1a(inputs, a_res, c_res):
    x = np.asarray(inputs["L0_attn_hidden_states"], np.float32).reshape(T, D)
    common = dict(
        wo=np.asarray(inputs["L0_attn_Wo"], np.float32),
        bo=np.asarray(inputs["L0_attn_bo"], np.float32)[None, :],
        eb=np.asarray(inputs["L0_ffn_expert_bias"], np.float32),
        wq1=np.asarray(inputs["L1_attn_wq"], np.float32),
        wk1=np.asarray(inputs["L1_attn_wk"], np.float32),
        wv1=np.asarray(inputs["L1_attn_wv"], np.float32),
        gwt=np.ascontiguousarray(np.asarray(inputs["L1_ffn_gate_w"], np.float32).T),
        ics=np.asarray(inputs["L1_ffn_in_channel_scale"], np.float32),
        ident=np.eye(128, dtype=np.float32),
    )
    nc = _get("l0d_l1a", build_l0d_l1a)
    in_maps = [dict(common, ot=c_res[c]["ot"], wmt=a_res[c]["wmt"], wsum=a_res[c]["wsum"],
                    mean=a_res[c]["mean"], rstd=a_res[c]["rstd"],
                    x=np.ascontiguousarray(x[c * TC:(c + 1) * TC])) for c in range(NCORE)]
    res = run_bass_kernel_spmd(nc, in_maps, core_ids=list(range(NCORE)))
    return res.results



def build_l1c():
    p = Prog("l1c")
    q1h = p.inp("q1h", [512, T], BF16)
    k1kv = p.inp("k1kv", [128, T], BF16)
    v1kv = p.inp("v1kv", [T, 128], BF16)
    identb = p.inp("identb", [128, 128], BF16)
    tri = p.inp("tri", [128, 128], BF16)
    ot1 = p.out("ot1", [512, T], BF16)

    kbuf = p.sbuf("kbuf", [128, T], BF16)
    vbuf = p.sbuf("vbuf", [128, 64, 130], BF16)
    qbuf = p.sbuf("qbuf", [128, 2, T], BF16)
    pt = p.sbuf("pt", [128, 2, 512], BF16)
    o_sb = p.sbuf("o_sb", [128, 2, 128], BF16)
    stage = p.sbuf("stage", [128, 4, 128], BF16)
    rec = p.sbuf("rec", [128, 4], F32)
    lcol = p.sbuf("lcol", [128, 4], F32)
    ident_sb = p.sbuf("ident_sb", [128, 128], BF16)
    tri_sb = p.sbuf("tri_sb", [128, 128], BF16)

    ps_s = p.psum("ps_s", [128, 2, 512], F32)
    ps_o = p.psum("ps_o", [128, 4, 512], F32)
    ps_t = p.psum("ps_t", [128, 2, 1024], BF16)

    s_in = p.sem("s_in")
    s_q = p.sem("s_q")
    s_pe = p.sem("s_pe")
    s_pv = p.sem("s_pv")
    s_act = p.sem("s_act")
    s_vec = p.sem("s_vec")
    s_st = p.sem("s_st")

    p.dma("gpsimd", ident_sb[:, :], identb, s_in)
    p.dma("gpsimd", tri_sb[:, :], tri, s_in)
    p.dma("sync", kbuf[:, :], k1kv, s_in)
    for q in range(4):
        p.dma("sync", vbuf[:, q * 16:(q + 1) * 16, 0:128],
              v1kv.rearrange("(kc p) n -> p kc n", p=128)[:, q * 16:(q + 1) * 16, :], s_in)
    consts = s_in.v
    p.run("vector", lambda e: e.memset(vbuf[:, :, 128:129], 1.0), s_vec)
    ones_done = s_vec.v

    q_guard = {0: 0, 1: 0}
    loads = {}

    def load_q(hl):
        b = hl % 2
        if q_guard[b] > 0:
            p.wait("sync", s_pv, q_guard[b])
        p.dma("sync", qbuf[:, b, :], q1h[hl * 128:(hl + 1) * 128, :], s_q)
        return s_q.v

    loads[0] = load_q(0)
    store_n = 0
    for hl in range(4):
        b = hl % 2
        if hl + 1 < 4:
            loads[hl + 1] = load_q(hl + 1)
        qv_ = loads[hl]
        exp_hist = []
        pv_hist = []
        for qg in range(16):
            nkc = 4 * qg + 4
            evac_guard = s_vec.v
            for kc in range(nkc):
                sbank = kc % 2
                if kc == 0:
                    p.wait("tensor", s_q, qv_)
                    if hl == 0 and qg == 0:
                        p.wait("tensor", s_in, consts)
                        p.wait("tensor", s_vec, ones_done)
                    p.wait("tensor", s_vec, evac_guard)
                if len(exp_hist) >= 2:
                    p.wait("tensor", s_act, exp_hist[-2])
                p.run("tensor", lambda e, kc=kc, qg=qg, b=b, sbank=sbank: e.matmul(
                    ps_s[:, sbank, :], kbuf[:, kc * 128:(kc + 1) * 128],
                    qbuf[:, b, qg * 512:(qg + 1) * 512], start=True, stop=True), s_pe, 1)
                qk_v = s_pe.v
                p.wait("scalar", s_pe, qk_v)
                if len(pv_hist) >= 2:
                    p.wait("scalar", s_pv, pv_hist[-2])
                p.run("scalar", lambda e, sbank=sbank: e.activation(
                    pt[:, sbank, :], ps_s[:, sbank, :], AF.Exp, scale=SCALE), s_act)
                exp_hist.append(s_act.v)
                d = kc - 4 * qg
                nmask = 0
                if d >= 0:
                    p.wait("vector", s_act, exp_hist[-1])
                    for j in range(4):
                        if d > j:
                            p.run("vector", lambda e, j=j, sbank=sbank: e.memset(pt[:, sbank, j * 128:(j + 1) * 128], 0.0), s_vec)
                            nmask += 1
                        elif d == j:
                            p.run("vector", lambda e, j=j, sbank=sbank: e.tensor_mul(
                                pt[:, sbank, j * 128:(j + 1) * 128], pt[:, sbank, j * 128:(j + 1) * 128], tri_sb[:, :]), s_vec)
                            nmask += 1
                    p.wait("tensor", s_vec)
                else:
                    p.wait("tensor", s_act, exp_hist[-1])
                for j in range(4):
                    p.run("tensor", lambda e, kc=kc, j=j, sbank=sbank, nkc=nkc: e.matmul(
                        ps_o[:, j, 0:129], pt[:, sbank, j * 128:(j + 1) * 128],
                        vbuf[:, kc, 0:129], start=(kc == 0), stop=(kc == nkc - 1)),
                        s_pv if j == 3 else None, 1)
                pv_hist.append(s_pv.v)
            pv_done = s_pv.v
            for j in range(4):
                jb = j % 2
                p.wait("vector", s_pv, pv_done)
                if j >= 2:
                    p.wait("vector", s_pe)
                p.run("vector", lambda e, j=j: e.tensor_copy(lcol[:, j:j + 1], ps_o[:, j, 128:129]), s_vec)
                p.wait("vector", s_vec)
                p.run("vector", lambda e, j=j: e.reciprocal(rec[:, j:j + 1], lcol[:, j:j + 1]), s_vec)
                p.wait("vector", s_vec)
                p.run("vector", lambda e, j=j, jb=jb: e.tensor_scalar_mul(
                    o_sb[:, jb, :], ps_o[:, j, 0:128], rec[:, j:j + 1]), s_vec)
                p.wait("tensor", s_vec)
                if j >= 2:
                    p.wait("tensor", s_act)
                p.run("tensor", lambda e, jb=jb: e.transpose(ps_t[:, jb, 0:128], o_sb[:, jb, :], ident_sb[:, :]), s_pe, 1)
                p.wait("scalar", s_pe)
                slot = store_n % 4
                need = 16 * (store_n - 3)
                if need > 0:
                    p.wait("scalar", s_st, need)
                store_n += 1
                p.run("scalar", lambda e, slot=slot, jb=jb: e.copy(stage[:, slot, :], ps_t[:, jb, 0:128]), s_act)
                p.wait("sync", s_act)
                p.dma("sync", ot1[hl * 128:(hl + 1) * 128, qg * 512 + j * 128:qg * 512 + (j + 1) * 128],
                      stage[:, slot, :], s_st)
        q_guard[b] = s_pv.v
    p.wait("sync", s_st)
    return p.build()


def run_l1c(d_res):
    qt1f = np.concatenate([d_res[c]["qt1"] for c in range(NCORE)], axis=1)
    kt1f = np.concatenate([d_res[c]["kt1"] for c in range(NCORE)], axis=1)
    v1f = np.concatenate([d_res[c]["v1"] for c in range(NCORE)], axis=0)
    tri = np.tril(np.ones((128, 128), dtype=np.float32)).T.astype(_bf16np())
    identb = np.eye(128, dtype=np.float32).astype(_bf16np())
    nc = _get("l1c", build_l1c)
    in_maps = [dict(q1h=np.ascontiguousarray(qt1f[c * 512:(c + 1) * 512]),
                    k1kv=np.ascontiguousarray(kt1f[c * 128:(c + 1) * 128]),
                    v1kv=np.ascontiguousarray(v1f[:, c * 128:(c + 1) * 128]),
                    identb=identb, tri=tri) for c in range(NCORE)]
    res = run_bass_kernel_spmd(nc, in_maps, core_ids=list(range(NCORE)))
    return res.results



def build_l1d():
    p = Prog("l1d")
    ot1tok = p.inp("ot1tok", [D, TC], BF16)
    wo1 = p.inp("wo1", [D, D], F32)
    wt1 = p.inp("wt1", [D, D], F32)
    b1 = p.inp("b1", [1, D], F32)
    ocs = p.inp("ocs", [8, D], F32)
    rtt_i = p.inp("rtt", [8, TC], BF16)
    xst_i = p.inp("xst", [D, TC], BF16)
    x1 = p.inp("x1", [TC, D], F32)
    x2 = p.out("x2", [TC, D], F32)

    SW = 256
    NSLAB = D // SW

    otld = p.sbuf("otld", [128, 2, 32, 128], BF16)
    xsld = p.sbuf("xsld", [128, 2, 32, 128], BF16)
    woslab = p.sbuf("woslab", [128, 2, 32, SW], BF16)
    wtraw = p.sbuf("wtraw", [128, 32, SW], BF16)
    wtsign = p.sbuf("wtsign", [128, 2, 32, SW], BF16)
    rtt_sb = p.sbuf("rtt_sb", [8, TC], BF16)
    ocs_sb = p.sbuf("ocs_sb", [8, D], BF16)
    b1_sb = p.sbuf("b1_sb", [1, D], BF16)
    b1rep = p.sbuf("b1rep", [128, D], F32)
    ones_k1 = p.sbuf("ones_k1", [1, 512], BF16)
    xp = p.sbuf("xp", [128, 2, SW], F32)
    os_sb = p.sbuf("os_sb", [128, 2, SW], F32)
    rsb = p.sbuf("rsb", [128, SW], F32)
    gsb = p.sbuf("gsb", [128, SW], F32)
    prsb = p.sbuf("prsb", [128, SW], F32)
    x2sb = p.sbuf("x2sb", [128, 2, SW], F32)

    ps_mm = p.psum("ps_mm", [128, 4, 512], F32)
    ps_b = p.psum("ps_b", [128, 2, 512], F32)

    s_in = p.sem("s_in")
    s_x = p.sem("s_x")
    s_w = p.sem("s_w")
    s_ld = p.sem("s_ld")
    s_pe = p.sem("s_pe")
    s_act = p.sem("s_act")
    s_vec = p.sem("s_vec")
    s_st = p.sem("s_st")

    p.dma("gpsimd", ocs_sb[:, :], ocs, s_in)
    p.dma("gpsimd", b1_sb[:, :], b1, s_in)
    p.dma("sync", rtt_sb[:, :], rtt_i, s_in)
    consts = s_in.v
    p.run("vector", lambda e: e.memset(ones_k1[:, :], 1.0), s_vec)
    vconst = s_vec.v

    for g in range(8):
        gb = g % 2
        p.wait("tensor", s_in, consts)
        p.wait("tensor", s_vec, vconst)
        if g >= 2:
            p.wait("tensor", s_vec)
        p.run("tensor", lambda e, g=g, gb=gb: e.matmul(ps_b[:, gb, :], ones_k1[0:1, 0:128], b1_sb[0:1, g * 512:(g + 1) * 512],
                                                       start=True, stop=True), s_pe, 1)
        p.wait("vector", s_pe)
        p.run("vector", lambda e, g=g, gb=gb: e.tensor_copy(b1rep[:, g * 512:(g + 1) * 512], ps_b[:, gb, :]), s_vec)
    b1rep_done = s_vec.v

    mmfree = {}
    bfree = {}
    wslab_guard = {0: 0, 1: 0}
    sraw_guard = [0]
    ld_guard = {0: 0, 1: 0}

    for sl in range(NSLAB):
        sb = sl % 2
        p.wait("gpsimd", s_pe, wslab_guard[sb])
        p.dma("gpsimd", woslab[:, sb, :, :], wo1[:, sl * SW:(sl + 1) * SW].rearrange("(c p) n -> p c n", p=128), s_w)
        p.wait("gpsimd", s_act, sraw_guard[0])
        p.dma("gpsimd", wtraw[:, :, :], wt1[:, sl * SW:(sl + 1) * SW].rearrange("(c p) n -> p c n", p=128), s_w)
        w_ready = s_w.v
        p.wait("scalar", s_w, w_ready)
        p.wait("scalar", s_pe, wslab_guard[sb])
        p.run("scalar", lambda e, sb=sb: e.activation(wtsign[:, sb, :, :].rearrange("p a b -> p (a b)"),
                                                      wtraw[:, :, :].rearrange("p a b -> p (a b)"), AF.Sign), s_act)
        sraw_guard[0] = s_act.v
        sign_done = s_act.v
        for t in range(NT):
            ob = t % 2
            p.wait("sync", s_pe, ld_guard[ob])
            p.dma("sync", otld[:, ob, :, :], ot1tok.rearrange("(c p) n -> p c n", p=128)[:, :, t * 128:(t + 1) * 128], s_ld)
            p.dma("sync", xsld[:, ob, :, :], xst_i.rearrange("(c p) n -> p c n", p=128)[:, :, t * 128:(t + 1) * 128], s_ld)
            ld_ready = s_ld.v
            xb = t % 2
            p.dma("sync", xp[:, xb, :], x1[t * 128:(t + 1) * 128, sl * SW:(sl + 1) * SW], s_x)
            x_ready = s_x.v
            sbank = t % 2
            fbank = 2 + t % 2
            p.wait("tensor", s_w, w_ready)
            p.wait("tensor", s_ld, ld_ready)
            p.wait("tensor", s_act, sign_done)
            if sbank in mmfree:
                p.wait("tensor", s_vec, mmfree[sbank])
            if fbank in mmfree:
                p.wait("tensor", s_vec, mmfree[fbank])
            for fc in range(32):
                p.run("tensor", lambda e, fc=fc, sb=sb, ob=ob, sbank=sbank: e.matmul(
                    ps_mm[:, sbank, 0:SW], otld[:, ob, fc, :], woslab[:, sb, fc, :],
                    start=(fc == 0), stop=(fc == 31)), s_pe if fc == 31 else None, 1)
            s_done = s_pe.v
            for fc in range(32):
                p.run("tensor", lambda e, fc=fc, sb=sb, ob=ob, fbank=fbank: e.matmul(
                    ps_mm[:, fbank, 0:SW], xsld[:, ob, fc, :], wtsign[:, sb, fc, :],
                    start=(fc == 0), stop=(fc == 31)), s_pe if fc == 31 else None, 1)
            f_done = s_pe.v
            ld_guard[ob] = s_pe.v
            bbank = t % 2
            if bbank in bfree:
                p.wait("tensor", s_act, bfree[bbank])
            else:
                p.wait("tensor", s_vec, b1rep_done)
            p.run("tensor", lambda e, bbank=bbank, t=t, sl=sl: e.matmul(
                ps_b[:, bbank, 0:SW], rtt_sb[:, t * 128:(t + 1) * 128], ocs_sb[:, sl * SW:(sl + 1) * SW],
                start=True, stop=True), s_pe, 1)
            os_done = s_pe.v
            p.wait("scalar", s_pe, os_done)
            p.run("scalar", lambda e, bbank=bbank: e.copy(os_sb[:, bbank, :], ps_b[:, bbank, 0:SW]), s_act)
            bfree[bbank] = s_act.v
            os_cp = s_act.v
            p.wait("vector", s_act, os_cp)
            p.wait("vector", s_pe, f_done)
            if sl == 0 and t == 0:
                p.wait("vector", s_vec, b1rep_done)
            p.run("vector", lambda e, fbank=fbank, bbank=bbank: e.tensor_mul(rsb[:, :], ps_mm[:, fbank, 0:SW], os_sb[:, bbank, :]), s_vec)
            p.wait("vector", s_vec)
            p.run("vector", lambda e, sl=sl: e.tensor_add(rsb[:, :], rsb[:, :], b1rep[:, sl * SW:(sl + 1) * SW]), s_vec)
            r2_done = s_vec.v
            mmfree[fbank] = s_vec.v
            p.wait("scalar", s_vec, r2_done)
            p.run("scalar", lambda e: e.activation(gsb[:, :], rsb[:, :], AF.Silu), s_act)
            p.wait("vector", s_act)
            p.run("vector", lambda e, sbank=sbank: e.tensor_mul(prsb[:, :], gsb[:, :], ps_mm[:, sbank, 0:SW]), s_vec)
            mmfree[sbank] = s_vec.v
            p.wait("vector", s_vec)
            p.wait("vector", s_x, x_ready)
            need = 16 * (p.store_n - 1) if p.store_n >= 2 else 0
            if need > 0:
                p.wait("vector", s_st, need)
            p.store_n += 1
            p.run("vector", lambda e, xb=xb: e.tensor_add(x2sb[:, xb, :], prsb[:, :], xp[:, xb, :]), s_vec)
            p.wait("sync", s_vec)
            p.dma("sync", x2[t * 128:(t + 1) * 128, sl * SW:(sl + 1) * SW], x2sb[:, xb, :], s_st)
        wslab_guard[sb] = s_pe.v
    p.wait("sync", s_st)
    return p.build()


def run_l1d(inputs, d_res, c1_res):
    ot1f = np.concatenate([c1_res[c]["ot1"] for c in range(NCORE)], axis=0)
    common = dict(
        wo1=np.asarray(inputs["L1_attn_wo"], np.float32),
        wt1=np.ascontiguousarray(np.asarray(inputs["L1_ffn_weight"], np.float32).T),
        b1=np.asarray(inputs["L1_ffn_bias"], np.float32)[None, :],
        ocs=np.asarray(inputs["L1_ffn_out_channel_scale"], np.float32),
    )
    nc = _get("l1d", build_l1d)
    in_maps = [dict(common,
                    ot1tok=np.ascontiguousarray(ot1f[:, c * TC:(c + 1) * TC]),
                    rtt=d_res[c]["rtt"], xst=d_res[c]["xst"], x1=d_res[c]["x1"]) for c in range(NCORE)]
    res = run_bass_kernel_spmd(nc, in_maps, core_ids=list(range(NCORE)))
    return res.results


def kernel(**inputs):
    a = run_l0a(inputs)
    cr = run_l0c(a)
    d = run_l0d_l1a(inputs, a, cr)
    c1 = run_l1c(d)
    e = run_l1d(inputs, d, c1)
    x2 = np.concatenate([e[c]["x2"] for c in range(NCORE)], axis=0)
    return x2.reshape(1, T, D)


# version 03
# speedup vs baseline: 2.0558x; 2.0558x over previous
import math
from contextlib import ExitStack

import numpy as np

import concourse.bass as bass
import concourse.mybir as mybir
from concourse.bass_utils import run_bass_kernel_spmd

F32 = mybir.dt.float32
BF16 = mybir.dt.bfloat16
AF = mybir.ActivationFunctionType
ALU = mybir.AluOpType
AX = mybir.AxisListType

NCORE = 8
T = 8192
TC = T // NCORE
D = 4096
NT = TC // 128
SCALE = 1.0 / math.sqrt(128.0)


class Sem:
    def __init__(self, h):
        self.h = h
        self.v = 0


class Prog:

    ENGINES = ("sync", "scalar", "vector", "tensor", "gpsimd")

    def __init__(self, name):
        self.name = name
        self.nc = bass.Bass("TRN2", target_bir_lowering=False)
        self.ops = {e: [] for e in self.ENGINES}
        self.ctx = ExitStack()
        self._guards = {}
        self.store_n = 0

    def sem(self, name):
        return Sem(self.ctx.enter_context(self.nc.semaphore(name)))

    def sbuf(self, name, shape, dt):
        g = self.nc.sbuf_tensor(name, shape, dt)
        t = g.__enter__()
        self._guards[name] = g
        return t

    def psum(self, name, shape, dt=F32):
        g = self.nc.psum_tensor(name, shape, dt)
        t = g.__enter__()
        self._guards[name] = g
        return t

    def free(self, *names):
        order = list(self._guards)
        for n in sorted(names, key=lambda x: -order.index(x)):
            self._guards.pop(n).__exit__(None, None, None)

    def inp(self, name, shape, dt):
        return self.nc.dram_tensor(name, shape, dt, kind="ExternalInput").ap()

    def out(self, name, shape, dt):
        return self.nc.dram_tensor(name, shape, dt, kind="ExternalOutput").ap()

    def on(self, eng, fn):
        self.ops[eng].append(fn)

    def wait(self, eng, sem, val=None):
        v = sem.v if val is None else val
        if v > 0:
            self.on(eng, lambda e, s=sem.h, v=v: e.wait_ge(s, v))

    def dma(self, eng, out, in_, sem, **kw):
        sem.v += 16
        self.on(eng, lambda e, o=out, i=in_, s=sem.h, kw=kw: e.dma_start(out=o, in_=i, **kw).then_inc(s, 16))

    def run(self, eng, fn, sem=None, n=1):
        if sem is None:
            self.on(eng, fn)
        else:
            sem.v += n
            self.on(eng, lambda e, f=fn, s=sem.h, n=n: f(e).then_inc(s, n))

    def build(self):
        for n in reversed(list(self._guards)):
            self._guards.pop(n).__exit__(None, None, None)
        block = self.ctx.enter_context(self.nc.Block())
        for name in self.ENGINES:
            ops = self.ops[name]
            if not ops:
                continue

            def runner(e, ops=ops):
                for f in ops:
                    f(e)

            getattr(block, name)(runner)
        self.ctx.close()
        return self.nc



def build_l0a(upto=5):
    p = Prog("l0a")
    x = p.inp("x", [TC, D], F32)
    wq = p.inp("wq", [D, D], F32)
    wk = p.inp("wk", [D, D], F32)
    wv = p.inp("wv", [D, D], F32)
    bias_in = p.inp("biases", [1, 3 * D + 64], F32)
    qraw_d = p.nc.dram_tensor("qraw_d", [D, TC], BF16, kind="Internal").ap()
    gqt = p.inp("gqt", [128, 32], F32)
    gkt = p.inp("gkt", [128, 32], F32)
    rwt = p.inp("rwt", [D, 64], F32)
    ident = p.inp("ident", [128, 128], F32)

    qt = p.out("qt", [D, TC], BF16)
    kt = p.out("kt", [D, TC], BF16)
    vout = p.out("v", [TC, D], BF16)
    wmt = p.out("wmt", [64, TC], BF16)
    wsum = p.out("wsum", [128, NT], F32)
    meano = p.out("mean", [128, NT], F32)
    rstdo = p.out("rstd", [128, NT], F32)

    SW = 256
    NSLAB = D // SW

    x_sb = p.sbuf("x_sb", [128, D], F32)
    sq_sb = p.sbuf("sq_sb", [128, 1024], F32)
    xn_sb = p.sbuf("xn_sb", [128, D], BF16)
    xnt = p.sbuf("xnt", [128, 32, TC], BF16)
    qld = p.sbuf("qld", [128, 2, 512], BF16)
    wslab = p.sbuf("wslab", [128, 2, 32, SW], BF16)
    ident_sb = p.sbuf("ident_sb", [128, 128], BF16)
    gq_sb = p.sbuf("gq_sb", [128, 32], F32)
    gk_sb = p.sbuf("gk_sb", [128, 32], F32)
    rwt_sb = p.sbuf("rwt_sb", [128, 32, 64], BF16)
    bias_sb = p.sbuf("bias_sb", [1, D], BF16)
    rb_sb = p.sbuf("rb_sb", [1, 64], BF16)
    ones_k1 = p.sbuf("ones_k1", [1, 512], BF16)
    ones_f = p.sbuf("ones_f", [128, 1], F32)
    ones_row_f = p.sbuf("ones_row_f", [1, 128], F32)
    mean_sb = p.sbuf("mean_sb", [128, NT], F32)
    rstd_sb = p.sbuf("rstd_sb", [128, NT], F32)
    negmr_sb = p.sbuf("negmr_sb", [128, NT], F32)
    wsum_sb = p.sbuf("wsum_sb", [128, NT], F32)
    st1 = p.sbuf("st1", [128, 4], F32)
    rrow = p.sbuf("rrow", [1, 2, TC], F32)
    rstdrep = p.sbuf("rstdrep", [128, TC], F32)
    stage = p.sbuf("stage", [128, 4, 512], BF16)
    sm_sb = p.sbuf("sm_sb", [128, 64], F32)
    top8 = p.sbuf("top8", [128, 8], F32)
    msk_sb = p.sbuf("msk_sb", [128, 64], F32)
    wm_bf = p.sbuf("wm_bf", [128, 64], BF16)
    selem = p.sbuf("selem", [128, 2], F32)
    rec_sb = p.sbuf("rec_sb", [128, 1], F32)
    eps_sb = p.sbuf("eps_sb", [128, 2], F32)

    ps_tr = p.psum("ps_tr", [128, 2, 1024], BF16)
    ps_mm = p.psum("ps_mm", [128, 4, 512], F32)
    ps_ss = p.psum("ps_ss", [1, 2, 512], F32)

    s_in = p.sem("s_in")
    s_x = p.sem("s_x")
    s_w = p.sem("s_w")
    s_pe = p.sem("s_pe")
    s_act = p.sem("s_act")
    s_vec = p.sem("s_vec")
    s_st = p.sem("s_st")
    s_ld = p.sem("s_ld")

    def stage_use(eng):
        slot = p.store_n % 4
        need = 16 * (p.store_n - 3)
        if need > 0:
            p.wait(eng, s_st, need)
        p.store_n += 1
        return slot

    p.dma("gpsimd", ident_sb[:, :], ident, s_in)
    p.dma("sync", gq_sb[:, :], gqt, s_in)
    p.dma("sync", gk_sb[:, :], gkt, s_in)
    p.dma("gpsimd", rwt_sb[:, :, :], rwt.rearrange("(c p) n -> p c n", p=128), s_in)
    p.dma("gpsimd", bias_sb[:, :], bias_in[0:1, 0:D], s_in)
    p.dma("gpsimd", rb_sb[:, :], bias_in[0:1, 3 * D:3 * D + 64], s_in)
    consts_done = s_in.v
    p.run("vector", lambda e: e.memset(ones_k1[:, :], 1.0), s_vec)
    p.run("vector", lambda e: e.memset(ones_f[:, :], 1.0), s_vec)
    p.run("vector", lambda e: e.memset(ones_row_f[:, :], 1.0), s_vec)
    p.run("vector", lambda e: e.memset(eps_sb[:, 0:1], 1e-5), s_vec)
    p.run("vector", lambda e: e.memset(eps_sb[:, 1:2], 1e-6), s_vec)
    vconst_done = s_vec.v

    for t in range(NT):
        if t > 0:
            p.wait("sync", s_act)
        p.dma("sync", x_sb[:, :], x[t * 128:(t + 1) * 128, :], s_x)
        xload = s_x.v
        p.wait("vector", s_x, xload)
        p.run("vector", lambda e: e.reduce_sum(st1[:, 0:1], x_sb[:, :], axis=AX.X), s_vec)
        for i in range(4):
            p.wait("scalar", s_x, xload)
            if i > 0:
                p.wait("scalar", s_vec)
            p.run("scalar", lambda e, i=i: e.activation(sq_sb[:, :], x_sb[:, i * 1024:(i + 1) * 1024], AF.Square), s_act)
            p.wait("vector", s_act)
            if i == 0:
                p.run("vector", lambda e: e.reduce_sum(st1[:, 1:2], sq_sb[:, :], axis=AX.X), s_vec)
            else:
                p.run("vector", lambda e: e.reduce_sum(st1[:, 2:3], sq_sb[:, :], axis=AX.X), s_vec)
                p.wait("vector", s_vec)
                p.run("vector", lambda e: e.tensor_add(st1[:, 1:2], st1[:, 1:2], st1[:, 2:3]), s_vec)
            p.wait("vector", s_vec)
        p.run("vector", lambda e, t=t: e.tensor_scalar_mul(mean_sb[:, t:t + 1], st1[:, 0:1], 1.0 / D), s_vec)
        p.wait("vector", s_vec)
        p.run("vector", lambda e, t=t: e.tensor_mul(st1[:, 3:4], mean_sb[:, t:t + 1], mean_sb[:, t:t + 1]), s_vec)
        p.wait("vector", s_vec)
        p.run("vector", lambda e: e.scalar_tensor_tensor(st1[:, 2:3], st1[:, 1:2], 1.0 / D, st1[:, 3:4], ALU.mult, ALU.subtract), s_vec)
        p.wait("scalar", s_vec)
        p.run("scalar", lambda e: e.activation(st1[:, 3:4], st1[:, 2:3], AF.Ln, bias=eps_sb[:, 0:1]), s_act)
        p.wait("scalar", s_act)
        p.run("scalar", lambda e, t=t: e.activation(rstd_sb[:, t:t + 1], st1[:, 3:4], AF.Exp, scale=-0.5), s_act)
        p.wait("vector", s_act)
        p.run("vector", lambda e, t=t: e.tensor_scalar(xn_sb[:, :], x_sb[:, :], mean_sb[:, t:t + 1], rstd_sb[:, t:t + 1], ALU.subtract, ALU.mult), s_vec)
        xn_ready = s_vec.v
        p.wait("tensor", s_vec, xn_ready)
        p.wait("tensor", s_in, consts_done)
        for fc in range(32):
            pb = fc % 2
            if fc >= 2:
                p.wait("tensor", s_act)
            p.run("tensor", lambda e, fc=fc, pb=pb: e.transpose(ps_tr[:, pb, 0:128], xn_sb[:, fc * 128:(fc + 1) * 128], ident_sb[:, :]), s_pe)
            p.wait("scalar", s_pe)
            p.run("scalar", lambda e, fc=fc, pb=pb, t=t: e.copy(xnt[:, fc, t * 128:(t + 1) * 128], ps_tr[:, pb, 0:128]), s_act)

    stats_act = s_act.v

    wslab_guard = {0: 0, 1: 0}
    bias_guard = [0]
    mmfree = {}

    def wait_bank(b):
        if b in mmfree:
            av, vv = mmfree[b]
            p.wait("tensor", s_act, av)
            p.wait("tensor", s_vec, vv)

    def gemm_qk(w, bias_off, out_ap, gain_sb, first):
        bankfree = mmfree
        p.wait("gpsimd", s_pe, bias_guard[0])
        p.dma("gpsimd", bias_sb[:, :], bias_in[0:1, bias_off:bias_off + D], s_in)
        bias_ready = s_in.v
        qd_guard = s_st.v
        for s in range(NSLAB):
            sb = s % 2
            p.wait("gpsimd", s_pe, wslab_guard[sb])
            p.dma("gpsimd", wslab[:, sb, :, :], w[:, s * SW:(s + 1) * SW].rearrange("(c p) n -> p c n", p=128), s_w)
            w_ready = s_w.v
            for ds in range(SW // 128):
                c = s * (SW // 128) + ds
                for g in range(2):
                    idx = c * 2 + g
                    pbank = idx % 4
                    p.wait("tensor", s_w, w_ready)
                    if first and idx == 0:
                        p.wait("tensor", s_act, stats_act)
                        p.wait("tensor", s_vec, vconst_done)
                    if idx == 0:
                        p.wait("tensor", s_in, bias_ready)
                    if pbank in bankfree:
                        av, vv = bankfree[pbank]
                        p.wait("tensor", s_act, av)
                        p.wait("tensor", s_vec, vv)
                    for fc in range(32):
                        p.run("tensor", lambda e, fc=fc, sb=sb, ds=ds, g=g, pbank=pbank: e.matmul(
                            ps_mm[:, pbank, :], wslab[:, sb, fc, ds * 128:(ds + 1) * 128],
                            xnt[:, fc, g * 512:(g + 1) * 512],
                            start=(fc == 0), stop=False), None)
                    p.run("tensor", lambda e, pbank=pbank, c=c: e.matmul(
                        ps_mm[:, pbank, :], bias_sb[0:1, c * 128:(c + 1) * 128],
                        ones_k1[0:1, :], start=False, stop=True), s_pe, 1)
                    mm_done = s_pe.v
                    p.wait("scalar", s_pe, mm_done)
                    p.run("scalar", lambda e, pbank=pbank: e.activation(sq_sb[:, 0:512], ps_mm[:, pbank, :], AF.Square), s_act)
                    sqd = s_act.v
                    p.wait("tensor", s_act, sqd)
                    p.run("tensor", lambda e, g=g, c=c: e.matmul(ps_ss[:, g, :], ones_f[:, :], sq_sb[:, 0:512],
                                                                 start=(c == 0), stop=(c == 31)), s_pe, 1)
                    slot = stage_use("scalar")
                    p.run("scalar", lambda e, slot=slot, pbank=pbank: e.copy(stage[:, slot, :], ps_mm[:, pbank, :]), s_act)
                    bankfree[pbank] = (s_act.v, 0)
                    p.wait("sync", s_act)
                    p.dma("sync", qraw_d.rearrange("(c p) n -> p c n", p=128)[:, c, g * 512:(g + 1) * 512],
                          stage[:, slot, :], s_st)
            wslab_guard[sb] = s_pe.v
        bias_guard[0] = s_pe.v
        ss_done = s_pe.v
        spill_done = s_st.v
        p.wait("scalar", s_pe, ss_done)
        p.run("scalar", lambda e: e.activation(rrow[0:1, 0, :], ps_ss[0:1, :, :].rearrange("p a b -> p (a b)"), AF.Ln, scale=1.0 / D, bias=eps_sb[0:1, 1:2]), s_act)
        p.wait("scalar", s_act)
        p.run("scalar", lambda e: e.activation(rrow[0:1, 1, :], rrow[0:1, 0, :], AF.Exp, scale=-0.5), s_act)
        rr_done = s_act.v
        for g in range(2):
            p.wait("tensor", s_act, rr_done)
            p.wait("tensor", s_vec, vconst_done)
            wait_bank(3)
            if g == 1:
                p.wait("tensor", s_vec)
            p.run("tensor", lambda e, g=g: e.matmul(ps_mm[:, 3, :], ones_row_f[0:1, :], rrow[0:1, 1, g * 512:(g + 1) * 512], start=True, stop=True), s_pe, 1)
            p.wait("vector", s_pe)
            p.run("vector", lambda e, g=g: e.tensor_copy(rstdrep[:, g * 512:(g + 1) * 512], ps_mm[:, 3, :]), s_vec)
        bankfree[3] = (s_act.v, s_vec.v)
        rep_done = s_vec.v
        ld_guard = {}
        for c in range(32):
            for g in range(2):
                idx = c * 2 + g
                lb = idx % 2
                if idx == 0:
                    p.wait("sync", s_st, spill_done)
                if lb in ld_guard:
                    p.wait("sync", s_vec, ld_guard[lb])
                p.dma("sync", qld[:, lb, :], qraw_d.rearrange("(c p) n -> p c n", p=128)[:, c, g * 512:(g + 1) * 512], s_ld)
                ld_v = s_ld.v
                slot = stage_use("vector")
                p.wait("vector", s_ld, ld_v)
                if idx == 0:
                    p.wait("vector", s_vec, rep_done)
                p.run("vector", lambda e, c=c, g=g, slot=slot, lb=lb, gain_sb=gain_sb: e.scalar_tensor_tensor(
                    stage[:, slot, :], qld[:, lb, :], gain_sb[:, c:c + 1],
                    rstdrep[:, g * 512:(g + 1) * 512], ALU.mult, ALU.mult), s_vec)
                ld_guard[lb] = s_vec.v
                p.wait("sync", s_vec)
                p.dma("sync", out_ap.rearrange("(c p) n -> p c n", p=128)[:, c, g * 512:(g + 1) * 512],
                      stage[:, slot, :], s_st)

    try:
        if upto >= 2:
            gemm_qk(wq, 0, qt, gq_sb, True)
        if upto >= 3:
            gemm_qk(wk, D, kt, gk_sb, False)

        if upto < 4:
            raise StopIteration
        p.wait("gpsimd", s_pe, bias_guard[0])
        p.dma("gpsimd", bias_sb[:, :], bias_in[0:1, 2 * D:3 * D], s_in)
        bv_ready = s_in.v
        for s in range(NSLAB):
            sb = s % 2
            p.wait("gpsimd", s_pe, wslab_guard[sb])
            p.dma("gpsimd", wslab[:, sb, :, :], wv[:, s * SW:(s + 1) * SW].rearrange("(c p) n -> p c n", p=128), s_w)
            w_ready = s_w.v
            for t in range(NT):
                pbank = t % 4
                p.wait("tensor", s_w, w_ready)
                wait_bank(pbank)
                for fc in range(32):
                    p.run("tensor", lambda e, fc=fc, sb=sb, t=t, pbank=pbank: e.matmul(
                        ps_mm[:, pbank, 0:SW], xnt[:, fc, t * 128:(t + 1) * 128],
                        wslab[:, sb, fc, :], start=(fc == 0), stop=False), None)
                if s == 0 and t == 0:
                    p.wait("tensor", s_in, bv_ready)
                p.run("tensor", lambda e, pbank=pbank, s=s: e.matmul(
                    ps_mm[:, pbank, 0:SW], ones_k1[0:1, 0:128], bias_sb[0:1, s * SW:(s + 1) * SW],
                    start=False, stop=True), s_pe, 1)
                mm_done = s_pe.v
                p.wait("scalar", s_pe, mm_done)
                slot = stage_use("scalar")
                p.run("scalar", lambda e, slot=slot, pbank=pbank: e.copy(stage[:, slot, 0:SW], ps_mm[:, pbank, 0:SW]), s_act)
                mmfree[pbank] = (s_act.v, 0)
                p.wait("sync", s_act)
                p.dma("sync", vout[t * 128:(t + 1) * 128, s * SW:(s + 1) * SW], stage[:, slot, 0:SW], s_st)
            wslab_guard[sb] = s_pe.v

        if upto < 5:
            raise StopIteration
        wait_bank(0)
        for t in range(NT):
            if t > 0:
                p.wait("tensor", s_act)
                p.wait("tensor", s_vec)
            for fc in range(32):
                p.run("tensor", lambda e, fc=fc, t=t: e.matmul(ps_mm[:, 0, 0:64], xnt[:, fc, t * 128:(t + 1) * 128],
                                                               rwt_sb[:, fc, :], start=(fc == 0), stop=False), None)
            p.run("tensor", lambda e: e.matmul(ps_mm[:, 0, 0:64], ones_k1[0:1, 0:128], rb_sb[0:1, :],
                                               start=False, stop=True), s_pe, 1)
            lg = s_pe.v
            p.wait("vector", s_pe, lg)
            p.run("vector", lambda e: e.reduce_max(selem[:, 0:1], ps_mm[:, 0, 0:64], axis=AX.X, negate=True), s_vec)
            p.wait("scalar", s_vec)
            p.run("scalar", lambda e: e.activation(sm_sb[:, :], ps_mm[:, 0, 0:64], AF.Exp, bias=selem[:, 0:1], accum_out=selem[:, 1:2]), s_act)
            p.wait("vector", s_act)
            p.run("vector", lambda e: e.reciprocal(rec_sb[:, :], selem[:, 1:2]), s_vec)
            p.wait("vector", s_vec)
            p.run("vector", lambda e: e.tensor_scalar_mul(sm_sb[:, :], sm_sb[:, :], rec_sb[:, 0:1]), s_vec)
            p.wait("vector", s_vec)
            p.run("vector", lambda e: e.max(out=top8[:, :], in_=sm_sb[:, :]), s_vec)
            p.wait("vector", s_vec)
            p.run("vector", lambda e, t=t: e.reduce_sum(wsum_sb[:, t:t + 1], top8[:, :], axis=AX.X), s_vec)
            p.run("vector", lambda e: e.tensor_single_scalar(msk_sb[:, :], sm_sb[:, :], top8[:, 7:8], ALU.is_ge), s_vec)
            p.wait("vector", s_vec)
            p.run("vector", lambda e: e.tensor_mul(wm_bf[:, :], sm_sb[:, :], msk_sb[:, :]), s_vec)
            wm_done = s_vec.v
            p.wait("tensor", s_vec, wm_done)
            if t > 0:
                p.wait("tensor", s_act)
            p.run("tensor", lambda e: e.transpose(ps_tr[0:64, 0, 0:128], wm_bf[:, :], ident_sb[:, :]), s_pe, 1)
            p.wait("scalar", s_pe)
            slot = stage_use("scalar")
            p.run("scalar", lambda e, slot=slot: e.copy(stage[0:64, slot, 0:128], ps_tr[0:64, 0, 0:128]), s_act)
            p.wait("sync", s_act)
            p.dma("sync", wmt[:, t * 128:(t + 1) * 128], stage[0:64, slot, 0:128], s_st)

    except StopIteration:
        pass

    p.wait("sync", s_vec)
    p.dma("sync", wsum, wsum_sb[:, :], s_st)
    p.dma("sync", meano, mean_sb[:, :], s_st)
    p.dma("sync", rstdo, rstd_sb[:, :], s_st)
    p.wait("sync", s_st)
    return p.build()



_NC_CACHE = {}


def _bf16np():
    import ml_dtypes
    return ml_dtypes.bfloat16


def _get(name, builder):
    if name not in _NC_CACHE:
        _NC_CACHE[name] = builder()
    return _NC_CACHE[name]


def run_l0a(inputs):
    x = np.asarray(inputs["L0_attn_hidden_states"], np.float32).reshape(T, D)
    biases = np.concatenate([
        np.asarray(inputs["L0_attn_bq"], np.float32),
        np.asarray(inputs["L0_attn_bk"], np.float32),
        np.asarray(inputs["L0_attn_bv"], np.float32),
        np.asarray(inputs["L0_ffn_router_bias"], np.float32),
    ])[None, :]
    common = dict(
        wq=np.asarray(inputs["L0_attn_Wq"], np.float32),
        wk=np.asarray(inputs["L0_attn_Wk"], np.float32),
        wv=np.asarray(inputs["L0_attn_Wv"], np.float32),
        biases=biases,
        gqt=np.ascontiguousarray(np.asarray(inputs["L0_attn_gq"], np.float32).reshape(32, 128).T),
        gkt=np.ascontiguousarray(np.asarray(inputs["L0_attn_gk"], np.float32).reshape(32, 128).T),
        rwt=np.ascontiguousarray(np.asarray(inputs["L0_ffn_router_weight"], np.float32).T),
        ident=np.eye(128, dtype=np.float32),
    )
    import os
    upto = int(os.environ.get("L0A_UPTO", "5"))
    nc = _get(f"l0a{upto}", lambda: build_l0a(upto))
    in_maps = [dict(common, x=np.ascontiguousarray(x[c * TC:(c + 1) * TC])) for c in range(NCORE)]
    res = run_bass_kernel_spmd(nc, in_maps, core_ids=list(range(NCORE)))
    return res.results



def build_l0c(nh=32):
    p = Prog("l0c")
    qt = p.inp("qt", [D, TC], BF16)
    ktf = p.inp("ktf", [D, T], BF16)
    vr = p.inp("vr", [32, 128, 64 * 130], BF16)
    identb = p.inp("identb", [128, 128], BF16)
    ot = p.out("ot", [D, TC], BF16)

    kbuf = p.sbuf("kbuf", [128, 2, T], BF16)
    vbuf = p.sbuf("vbuf", [128, 2, 64, 130], BF16)
    qbuf = p.sbuf("qbuf", [128, 2, TC], BF16)
    pt = p.sbuf("pt", [128, 2, 2, 512], BF16)
    o_sb = p.sbuf("o_sb", [128, 4, 128], BF16)
    otst = p.sbuf("otst", [128, 4, 128], BF16)
    rec = p.sbuf("rec", [128, 4], F32)
    lcol = p.sbuf("lcol", [128, 4], F32)
    ident_sb = p.sbuf("ident_sb", [128, 128], BF16)

    ps_s = p.psum("ps_s", [128, 4, 512], F32)
    ps_o = p.psum("ps_o", [128, 4, 512], F32)

    s_in_c = p.sem("s_in_c")
    s_k = p.sem("s_k")
    s_v = p.sem("s_v")
    s_q = p.sem("s_q")
    s_pe = p.sem("s_pe")
    s_pv = p.sem("s_pv")
    s_act = p.sem("s_act")
    s_vec = p.sem("s_vec")
    s_st = p.sem("s_st")

    p.dma("gpsimd", ident_sb[:, :], identb, s_in_c)
    idv = s_in_c.v
    ones_done = s_vec.v

    def issue_loads(h, guard_pe):
        b = h % 2
        if guard_pe > 0:
            p.wait("sync", s_pv, guard_pe)
        p.dma("sync", kbuf[:, b, :], ktf[h * 128:(h + 1) * 128, :], s_k)
        p.dma("sync", vbuf[:, b, :, :].rearrange("p a c -> p (a c)"), vr[h, :, :], s_v)
        p.dma("sync", qbuf[:, b, :], qt[h * 128:(h + 1) * 128, :], s_q)
        return s_k.v, s_v.v, s_q.v

    buf_guard = {0: 0, 1: 0}
    loads = {}
    loads[0] = issue_loads(0, 0)

    NP = 32

    for h in range(nh):
        b = h % 2
        if h + 1 < nh:
            loads[h + 1] = issue_loads(h + 1, buf_guard[(h + 1) % 2])
        kv_, vv_, qv_ = loads[h]
        evac_act = [0]
        for qg in range(2):
            evac_guard = s_vec.v
            exp_hist = []
            pv_hist = []

            def emit_qk(kp):
                pb = kp % 2
                if kp == 0:
                    p.wait("tensor", s_k, kv_)
                    p.wait("tensor", s_q, qv_)
                    p.wait("tensor", s_v, vv_)
                    p.wait("tensor", s_vec, evac_guard)
                if kp == 0:
                    p.wait("tensor", s_act, evac_act[0])
                if len(exp_hist) >= 2:
                    p.wait("tensor", s_act, exp_hist[-2])
                for half in range(2):
                    kc = 2 * kp + half
                    p.run("tensor", lambda e, b=b, kc=kc, qg=qg, pb=pb, half=half: e.matmul(
                        ps_s[:, 2 * pb + half, :], kbuf[:, b, kc * 128:(kc + 1) * 128],
                        qbuf[:, b, qg * 512:(qg + 1) * 512], start=True, stop=True),
                        s_pe if half == 1 else None, 1)
                p.wait("scalar", s_pe, s_pe.v)
                if len(pv_hist) >= 2:
                    p.wait("scalar", s_pv, pv_hist[-2])
                p.run("scalar", lambda e, pb=pb: e.activation(
                    pt[:, pb, :, :], ps_s[:, 2 * pb:2 * pb + 2, :], AF.Exp, scale=SCALE), s_act)
                exp_hist.append(s_act.v)

            def emit_pv(kp, last):
                pb = kp % 2
                p.wait("tensor", s_act, exp_hist[kp])
                for half in range(2):
                    kc = 2 * kp + half
                    for j in range(4):
                        p.run("tensor", lambda e, b=b, kc=kc, j=j, pb=pb, half=half: e.matmul(
                            ps_o[:, j, 0:129], pt[:, pb, half, j * 128:(j + 1) * 128],
                            vbuf[:, b, kc, 0:129], start=(kc == 0), stop=(kc == 63)),
                            s_pv if (half == 1 and j == 3) else None, 1)
                pv_hist.append(s_pv.v)

            emit_qk(0)
            for kp in range(1, NP):
                emit_qk(kp)
                emit_pv(kp - 1, False)
            emit_pv(NP - 1, True)

            pv_done = s_pv.v
            tr_copies = []
            for j in range(4):
                p.wait("vector", s_pv, pv_done)
                if j == 0:
                    need = 16 * (p.store_n - 3)
                    if need > 0:
                        p.wait("vector", s_st, need)
                p.run("vector", lambda e, j=j: e.tensor_copy(lcol[:, j:j + 1], ps_o[:, j, 128:129]), s_vec)
                p.wait("vector", s_vec)
                p.run("vector", lambda e, j=j: e.reciprocal(rec[:, j:j + 1], lcol[:, j:j + 1]), s_vec)
                p.wait("vector", s_vec)
                p.run("vector", lambda e, j=j: e.tensor_scalar_mul(
                    o_sb[:, j, :], ps_o[:, j, 0:128], rec[:, j:j + 1]), s_vec)
                p.wait("tensor", s_vec)
                if h == 0 and qg == 0 and j == 0:
                    p.wait("tensor", s_in_c, idv)
                p.run("tensor", lambda e, j=j: e.matmul(ps_s[:, j, 0:128], o_sb[:, j, :], ident_sb[:, :],
                                                        start=True, stop=True), s_pe, 1)
                p.wait("scalar", s_pe)
                p.run("scalar", lambda e, j=j: e.copy(otst[:, j, :], ps_s[:, j, 0:128]), s_act)
                tr_copies.append(s_act.v)
                p.wait("sync", s_act)
                p.store_n += 1
                p.dma("sync", ot[h * 128:(h + 1) * 128, qg * 512 + j * 128:qg * 512 + (j + 1) * 128],
                      otst[:, j, :], s_st)
            evac_act[0] = s_act.v
        buf_guard[b] = s_pv.v
    p.wait("sync", s_st)
    return p.build()


def run_l0c(l0a_res):
    ktf = np.concatenate([l0a_res[c]["kt"] for c in range(NCORE)], axis=1)
    vf = np.concatenate([l0a_res[c]["v"] for c in range(NCORE)], axis=0)
    v4 = vf.reshape(64, 128, 32, 128).transpose(2, 1, 0, 3)
    pad = np.zeros((32, 128, 64, 2), dtype=v4.dtype)
    pad[:, :, :, 0] = np.float32(1.0)
    vrr = np.ascontiguousarray(np.concatenate([v4, pad], axis=3).reshape(32, 128, 64 * 130))
    import os
    nh = int(os.environ.get("L0C_NH", "32"))
    nc = _get(f"l0c{nh}", lambda: build_l0c(nh))
    identb = np.eye(128, dtype=np.float32).astype(_bf16np())
    in_maps = [dict(qt=l0a_res[c]["qt"], ktf=ktf, vr=vrr, identb=identb) for c in range(NCORE)]
    res = run_bass_kernel_spmd(nc, in_maps, core_ids=list(range(NCORE)))
    return res.results



def build_l0d_l1a():
    p = Prog("l0d_l1a")
    ot = p.inp("ot", [D, TC], BF16)
    wo = p.inp("wo", [D, D], F32)
    bo = p.inp("bo", [1, D], F32)
    eb = p.inp("eb", [64, D], F32)
    wmt_i = p.inp("wmt", [64, TC], BF16)
    wsum_i = p.inp("wsum", [128, NT], F32)
    mean_i = p.inp("mean", [128, NT], F32)
    rstd_i = p.inp("rstd", [128, NT], F32)
    x = p.inp("x", [TC, D], F32)
    wq1 = p.inp("wq1", [D, D], F32)
    wk1 = p.inp("wk1", [D, 1024], F32)
    wv1 = p.inp("wv1", [D, 1024], F32)
    gwt = p.inp("gwt", [D, 8], F32)
    ics = p.inp("ics", [8, D], F32)
    ident = p.inp("ident", [128, 128], F32)

    x1 = p.out("x1", [TC, D], F32)
    x1_d = p.nc.dram_tensor("x1_d", [TC, D], F32, kind="Internal").ap()
    qt1 = p.out("qt1", [D, TC], BF16)
    kt1 = p.out("kt1", [1024, TC], BF16)
    v1 = p.out("v1", [TC, 1024], BF16)
    rtt = p.out("rtt", [8, TC], BF16)
    xst = p.out("xst", [D, TC], BF16)

    SW = 256
    NSLAB = D // SW

    wslab = p.sbuf("wslab", [128, 2, 32, SW], BF16)
    wmt_sb = p.sbuf("wmt_sb", [64, TC], BF16)
    wsum_sb = p.sbuf("wsum_sb", [128, NT], F32)
    mean_sb = p.sbuf("mean_sb", [128, NT], F32)
    rstd_sb = p.sbuf("rstd_sb", [128, NT], F32)
    ident_sb = p.sbuf("ident_sb", [128, 128], BF16)
    ones_k1 = p.sbuf("ones_k1", [1, 512], BF16)
    
    ps_mm = p.psum("ps_mm", [128, 4, 512], F32)
    ps_b = p.psum("ps_b", [128, 2, 512], F32)
    ps_tr = p.psum("ps_tr", [128, 2, 1024], BF16)

    s_in = p.sem("s_in")
    s_x = p.sem("s_x")
    s_w = p.sem("s_w")
    s_ot = p.sem("s_ot")
    s_pe = p.sem("s_pe")
    s_act = p.sem("s_act")
    s_vec = p.sem("s_vec")
    s_st = p.sem("s_st")

    def stage_use(eng):
        slot = p.store_n % 4
        need = 16 * (p.store_n - 3)
        if need > 0:
            p.wait(eng, s_st, need)
        p.store_n += 1
        return slot

    p.dma("gpsimd", ident_sb[:, :], ident, s_in)
    p.dma("sync", wmt_sb[:, :], wmt_i, s_in)
    p.dma("sync", wsum_sb[:, :], wsum_i, s_in)
    p.dma("sync", mean_sb[:, :], mean_i, s_in)
    p.dma("sync", rstd_sb[:, :], rstd_i, s_in)
    consts = s_in.v
    p.run("vector", lambda e: e.memset(ones_k1[:, :], 1.0), s_vec)
    vconst = s_vec.v

    mmfree = {}
    bfree = {}

    def wait_bank(b, eng="tensor"):
        if b in mmfree:
            p.wait(eng, s_vec, mmfree[b])

    otres = p.sbuf("otres", [128, 32, TC], BF16)
    eb_sb = p.sbuf("eb_sb", [64, 2, SW], BF16)
    bo_sb = p.sbuf("bo_sb", [1, 2, SW], BF16)
    xp = p.sbuf("xp", [128, 2, SW], F32)
    xnf = p.sbuf("xnf", [128, SW], F32)
    fsb = p.sbuf("fsb", [128, SW], F32)
    gsb = p.sbuf("gsb", [128, SW], F32)
    prsb = p.sbuf("prsb", [128, SW], F32)
    x1sb = p.sbuf("x1sb", [128, 2, SW], F32)

    p.dma("sync", otres[:, :, :], ot.rearrange("(c p) n -> p c n", p=128), s_ot)
    otres_v = s_ot.v
    wslab_guard = {0: 0, 1: 0}
    x1_dmas = {}
    xp_guard = {}
    first_piece = True
    for sl in range(NSLAB):
        sb = sl % 2
        p.wait("gpsimd", s_pe, wslab_guard[sb])
        p.dma("gpsimd", wslab[:, sb, :, :], wo[:, sl * SW:(sl + 1) * SW].rearrange("(c p) n -> p c n", p=128), s_w)
        p.dma("gpsimd", eb_sb[:, sb, :], eb[:, sl * SW:(sl + 1) * SW], s_w)
        p.dma("gpsimd", bo_sb[:, sb, :], bo[:, sl * SW:(sl + 1) * SW], s_w)
        w_ready = s_w.v
        for t in range(NT):
            xb = t % 2
            if xb in xp_guard:
                p.wait("sync", s_vec, xp_guard[xb])
            p.dma("sync", xp[:, xb, :], x[t * 128:(t + 1) * 128, sl * SW:(sl + 1) * SW], s_x)
            x_ready = s_x.v
            pbank = t % 2
            p.wait("tensor", s_w, w_ready)
            if first_piece:
                p.wait("tensor", s_ot, otres_v)
                p.wait("tensor", s_in, consts)
                p.wait("tensor", s_vec, vconst)
                first_piece = False
            wait_bank(pbank)
            for fc in range(32):
                p.run("tensor", lambda e, fc=fc, sb=sb, t=t, pbank=pbank: e.matmul(
                    ps_mm[:, pbank, 0:SW], otres[:, fc, t * 128:(t + 1) * 128], wslab[:, sb, fc, :],
                    start=(fc == 0), stop=False), None)
            p.run("tensor", lambda e, pbank=pbank, sb=sb: e.matmul(
                ps_mm[:, pbank, 0:SW], ones_k1[0:1, 0:128], bo_sb[0:1, sb, :],
                start=False, stop=True), s_pe, 1)
            s_done = s_pe.v
            bbank = t % 2
            if bbank in bfree:
                p.wait("tensor", s_vec, bfree[bbank])
            p.run("tensor", lambda e, bbank=bbank, t=t, sb=sb: e.matmul(
                ps_b[:, bbank, 0:SW], wmt_sb[:, t * 128:(t + 1) * 128], eb_sb[:, sb, :],
                start=True, stop=True), s_pe, 1)
            b_done = s_pe.v
            p.wait("vector", s_x, x_ready)
            p.run("vector", lambda e, xb=xb, t=t: e.tensor_scalar(
                xnf[:, :], xp[:, xb, :], mean_sb[:, t:t + 1], rstd_sb[:, t:t + 1], ALU.subtract, ALU.mult), s_vec)
            p.wait("vector", s_vec)
            p.wait("vector", s_pe, b_done)
            p.run("vector", lambda e, t=t, bbank=bbank: e.scalar_tensor_tensor(
                fsb[:, :], xnf[:, :], wsum_sb[:, t:t + 1], ps_b[:, bbank, 0:SW], ALU.mult, ALU.add), s_vec)
            bfree[bbank] = s_vec.v
            p.wait("scalar", s_vec)
            p.run("scalar", lambda e: e.activation(gsb[:, :], fsb[:, :], AF.Silu), s_act)
            p.wait("vector", s_act)
            p.run("vector", lambda e, pbank=pbank: e.tensor_mul(prsb[:, :], gsb[:, :], ps_mm[:, pbank, 0:SW]), s_vec)
            mmfree[pbank] = s_vec.v
            p.wait("vector", s_vec)
            need = 16 * (x1_dmas[xb] if xb in x1_dmas else 0)
            if need > 0:
                p.wait("vector", s_st, need)
            p.run("vector", lambda e, xb=xb: e.tensor_add(x1sb[:, xb, :], prsb[:, :], xp[:, xb, :]), s_vec)
            xp_guard[xb] = s_vec.v
            p.wait("sync", s_vec)
            p.dma("sync", x1[t * 128:(t + 1) * 128, sl * SW:(sl + 1) * SW], x1sb[:, xb, :], s_st)
            p.dma("sync", x1_d[t * 128:(t + 1) * 128, sl * SW:(sl + 1) * SW], x1sb[:, xb, :], s_st)
            x1_dmas[xb] = s_st.v // 16
        wslab_guard[sb] = s_pe.v
    x1_done = s_st.v

    p.free("otres", "eb_sb", "bo_sb", "xp", "xnf", "fsb", "gsb", "prsb", "x1sb")
    x1t_sb = p.sbuf("x1t_sb", [128, D], F32)
    sq_sb = p.sbuf("sq_sb", [128, 512], F32)
    xn1_sb = p.sbuf("xn1_sb", [128, D], BF16)
    xs_sb = p.sbuf("xs_sb", [128, D], BF16)
    xnt1 = p.sbuf("xnt1", [128, 32, TC], BF16)
    gwt_sb = p.sbuf("gwt_sb", [128, 32, 8], BF16)
    ics_sb = p.sbuf("ics_sb", [8, D], BF16)
    st1 = p.sbuf("st1", [128, 4], F32)
    mean1_sb = p.sbuf("mean1_sb", [128, NT], F32)
    rstd1_sb = p.sbuf("rstd1_sb", [128, NT], F32)
    rt_f = p.sbuf("rt_f", [128, 8], F32)
    rt_bf = p.sbuf("rt_bf", [128, 8], BF16)
    rtT_sb = p.sbuf("rtT_sb", [8, TC], BF16)
    selem = p.sbuf("selem", [128, 2], F32)
    rec_sb = p.sbuf("rec_sb", [128, 1], F32)
    eps_sb = p.sbuf("eps_sb", [128, 1], F32)
    stage = p.sbuf("stage", [128, 4, 512], BF16)
    p.dma("gpsimd", gwt_sb[:, :, :], gwt.rearrange("(c p) n -> p c n", p=128), s_in)
    p.dma("gpsimd", ics_sb[:, :], ics, s_in)
    l1a_consts = s_in.v
    p.run("vector", lambda e: e.memset(eps_sb[:, :], 1e-5), s_vec)
    eps_done = s_vec.v

    for t in range(NT):
        if t > 0:
            p.wait("sync", s_act)
            p.wait("sync", s_vec)
        if t == 0:
            p.wait("sync", s_st, x1_done)
        p.dma("sync", x1t_sb[:, :], x1_d[t * 128:(t + 1) * 128, :], s_x)
        xload = s_x.v
        p.wait("vector", s_x, xload)
        p.run("vector", lambda e: e.reduce_sum(st1[:, 0:1], x1t_sb[:, :], axis=AX.X), s_vec)
        for i in range(8):
            p.wait("scalar", s_x, xload)
            if i > 0:
                p.wait("scalar", s_vec)
            p.run("scalar", lambda e, i=i: e.activation(sq_sb[:, :], x1t_sb[:, i * 512:(i + 1) * 512], AF.Square), s_act)
            p.wait("vector", s_act)
            if i == 0:
                p.run("vector", lambda e: e.reduce_sum(st1[:, 1:2], sq_sb[:, :], axis=AX.X), s_vec)
            else:
                p.run("vector", lambda e: e.reduce_sum(st1[:, 2:3], sq_sb[:, :], axis=AX.X), s_vec)
                p.wait("vector", s_vec)
                p.run("vector", lambda e: e.tensor_add(st1[:, 1:2], st1[:, 1:2], st1[:, 2:3]), s_vec)
            p.wait("vector", s_vec)
        p.run("vector", lambda e, t=t: e.tensor_scalar_mul(mean1_sb[:, t:t + 1], st1[:, 0:1], 1.0 / D), s_vec)
        p.wait("vector", s_vec)
        p.run("vector", lambda e, t=t: e.tensor_mul(st1[:, 3:4], mean1_sb[:, t:t + 1], mean1_sb[:, t:t + 1]), s_vec)
        p.wait("vector", s_vec)
        p.run("vector", lambda e: e.scalar_tensor_tensor(st1[:, 2:3], st1[:, 1:2], 1.0 / D, st1[:, 3:4], ALU.mult, ALU.subtract), s_vec)
        p.wait("scalar", s_vec)
        if t == 0:
            p.wait("scalar", s_vec, eps_done)
        p.run("scalar", lambda e: e.activation(st1[:, 3:4], st1[:, 2:3], AF.Ln, bias=eps_sb[:, 0:1]), s_act)
        p.wait("scalar", s_act)
        p.run("scalar", lambda e, t=t: e.activation(rstd1_sb[:, t:t + 1], st1[:, 3:4], AF.Exp, scale=-0.5), s_act)
        p.wait("vector", s_act)
        p.run("vector", lambda e, t=t: e.tensor_scalar(xn1_sb[:, :], x1t_sb[:, :], mean1_sb[:, t:t + 1], rstd1_sb[:, t:t + 1], ALU.subtract, ALU.mult), s_vec)
        xn_ready = s_vec.v
        p.wait("tensor", s_vec, xn_ready)
        for fc in range(32):
            pb = fc % 2
            if fc >= 2:
                p.wait("tensor", s_act)
            p.run("tensor", lambda e, fc=fc, pb=pb: e.transpose(ps_tr[:, pb, 0:128], xn1_sb[:, fc * 128:(fc + 1) * 128], ident_sb[:, :]), s_pe)
            p.wait("scalar", s_pe)
            p.run("scalar", lambda e, fc=fc, pb=pb, t=t: e.copy(xnt1[:, fc, t * 128:(t + 1) * 128], ps_tr[:, pb, 0:128]), s_act)
        tr_done = s_act.v
        p.wait("tensor", s_act, tr_done)
        p.wait("tensor", s_in, l1a_consts)
        wait_bank(0)
        for fc in range(32):
            p.run("tensor", lambda e, fc=fc, t=t: e.matmul(ps_mm[:, 0, 0:8], xnt1[:, fc, t * 128:(t + 1) * 128],
                                                           gwt_sb[:, fc, :], start=(fc == 0), stop=(fc == 31)),
                  s_pe if fc == 31 else None, 1)
        lg = s_pe.v
        p.wait("vector", s_pe, lg)
        p.run("vector", lambda e: e.reduce_max(selem[:, 0:1], ps_mm[:, 0, 0:8], axis=AX.X, negate=True), s_vec)
        p.wait("scalar", s_vec)
        p.run("scalar", lambda e: e.activation(rt_f[:, :], ps_mm[:, 0, 0:8], AF.Exp, bias=selem[:, 0:1], accum_out=selem[:, 1:2]), s_act)
        p.wait("vector", s_act)
        p.run("vector", lambda e: e.reciprocal(rec_sb[:, :], selem[:, 1:2]), s_vec)
        p.wait("vector", s_vec)
        p.run("vector", lambda e: e.tensor_scalar_mul(rt_bf[:, :], rt_f[:, :], rec_sb[:, 0:1]), s_vec)
        mmfree[0] = s_vec.v
        rt_done = s_vec.v
        p.wait("tensor", s_vec, rt_done)
        p.wait("tensor", s_act)
        p.run("tensor", lambda e: e.transpose(ps_tr[0:8, 0, 0:128], rt_bf[:, :], ident_sb[:, :]), s_pe, 1)
        p.wait("scalar", s_pe)
        p.run("scalar", lambda e, t=t: e.copy(rtT_sb[:, t * 128:(t + 1) * 128], ps_tr[0:8, 0, 0:128]), s_act)
        rtT_done = s_act.v
        for sl2 in range(8):
            p.wait("tensor", s_act, rtT_done)
            bbank = sl2 % 2
            if bbank in bfree:
                p.wait("tensor", s_vec, bfree[bbank])
            p.run("tensor", lambda e, sl2=sl2, t=t, bbank=bbank: e.matmul(
                ps_b[:, bbank, :], rtT_sb[:, t * 128:(t + 1) * 128], ics_sb[:, sl2 * 512:(sl2 + 1) * 512],
                start=True, stop=True), s_pe, 1)
            isd = s_pe.v
            p.wait("vector", s_pe, isd)
            p.run("vector", lambda e, sl2=sl2, bbank=bbank: e.tensor_mul(
                xs_sb[:, sl2 * 512:(sl2 + 1) * 512], xn1_sb[:, sl2 * 512:(sl2 + 1) * 512], ps_b[:, bbank, :]), s_vec)
            bfree[bbank] = s_vec.v
        xs_done = s_vec.v
        p.wait("tensor", s_vec, xs_done)
        for fc in range(32):
            pb = fc % 2
            p.wait("tensor", s_act)
            p.run("tensor", lambda e, fc=fc, pb=pb: e.transpose(ps_tr[:, pb, 0:128], xs_sb[:, fc * 128:(fc + 1) * 128], ident_sb[:, :]), s_pe)
            p.wait("scalar", s_pe)
            slot = stage_use("scalar")
            p.run("scalar", lambda e, slot=slot, pb=pb: e.copy(stage[:, slot, 0:128], ps_tr[:, pb, 0:128]), s_act)
            p.wait("sync", s_act)
            p.dma("sync", xst[fc * 128:(fc + 1) * 128, t * 128:(t + 1) * 128], stage[:, slot, 0:128], s_st)
    p.wait("sync", s_act)
    p.dma("sync", rtt, rtT_sb[:, :], s_st)

    def gemm_plain(w, out_ap, ncol, featmajor):
        nslab = ncol // SW
        for sl in range(nslab):
            sb = sl % 2
            p.wait("gpsimd", s_pe, wslab_guard[sb])
            p.dma("gpsimd", wslab[:, sb, :, :], w[:, sl * SW:(sl + 1) * SW].rearrange("(c p) n -> p c n", p=128), s_w)
            w_ready = s_w.v
            if featmajor:
                for ds in range(SW // 128):
                    c = sl * (SW // 128) + ds
                    for g in range(2):
                        pbank = (c * 2 + g) % 4
                        p.wait("tensor", s_w, w_ready)
                        wait_bank(pbank)
                        for fc in range(32):
                            p.run("tensor", lambda e, fc=fc, sb=sb, ds=ds, g=g, pbank=pbank: e.matmul(
                                ps_mm[:, pbank, :], wslab[:, sb, fc, ds * 128:(ds + 1) * 128],
                                xnt1[:, fc, g * 512:(g + 1) * 512],
                                start=(fc == 0), stop=(fc == 31)),
                                  s_pe if fc == 31 else None, 1)
                        mm = s_pe.v
                        p.wait("vector", s_pe, mm)
                        slot = stage_use("vector")
                        p.run("vector", lambda e, slot=slot, pbank=pbank: e.tensor_copy(stage[:, slot, :], ps_mm[:, pbank, :]), s_vec)
                        mmfree[pbank] = s_vec.v
                        p.wait("sync", s_vec)
                        p.dma("sync", out_ap.rearrange("(c p) n -> p c n", p=128)[:, c, g * 512:(g + 1) * 512],
                              stage[:, slot, :], s_st)
            else:
                for t in range(NT):
                    pbank = t % 4
                    p.wait("tensor", s_w, w_ready)
                    wait_bank(pbank)
                    for fc in range(32):
                        p.run("tensor", lambda e, fc=fc, sb=sb, t=t, pbank=pbank: e.matmul(
                            ps_mm[:, pbank, 0:SW], xnt1[:, fc, t * 128:(t + 1) * 128],
                            wslab[:, sb, fc, :], start=(fc == 0), stop=(fc == 31)),
                              s_pe if fc == 31 else None, 1)
                    mm = s_pe.v
                    p.wait("vector", s_pe, mm)
                    slot = stage_use("vector")
                    p.run("vector", lambda e, slot=slot, pbank=pbank: e.tensor_copy(stage[:, slot, 0:SW], ps_mm[:, pbank, 0:SW]), s_vec)
                    mmfree[pbank] = s_vec.v
                    p.wait("sync", s_vec)
                    p.dma("sync", out_ap[t * 128:(t + 1) * 128, sl * SW:(sl + 1) * SW], stage[:, slot, 0:SW], s_st)
            wslab_guard[sb] = s_pe.v

    gemm_plain(wq1, qt1, D, True)
    gemm_plain(wk1, kt1, 1024, True)
    gemm_plain(wv1, v1, 1024, False)

    p.wait("sync", s_st)
    return p.build()


def run_l0d_l1a(inputs, a_res, c_res):
    x = np.asarray(inputs["L0_attn_hidden_states"], np.float32).reshape(T, D)
    common = dict(
        wo=np.asarray(inputs["L0_attn_Wo"], np.float32),
        bo=np.asarray(inputs["L0_attn_bo"], np.float32)[None, :],
        eb=np.asarray(inputs["L0_ffn_expert_bias"], np.float32),
        wq1=np.asarray(inputs["L1_attn_wq"], np.float32),
        wk1=np.asarray(inputs["L1_attn_wk"], np.float32),
        wv1=np.asarray(inputs["L1_attn_wv"], np.float32),
        gwt=np.ascontiguousarray(np.asarray(inputs["L1_ffn_gate_w"], np.float32).T),
        ics=np.asarray(inputs["L1_ffn_in_channel_scale"], np.float32),
        ident=np.eye(128, dtype=np.float32),
    )
    nc = _get("l0d_l1a", build_l0d_l1a)
    in_maps = [dict(common, ot=c_res[c]["ot"], wmt=a_res[c]["wmt"], wsum=a_res[c]["wsum"],
                    mean=a_res[c]["mean"], rstd=a_res[c]["rstd"],
                    x=np.ascontiguousarray(x[c * TC:(c + 1) * TC])) for c in range(NCORE)]
    res = run_bass_kernel_spmd(nc, in_maps, core_ids=list(range(NCORE)))
    return res.results



def build_l1c():
    p = Prog("l1c")
    q1h = p.inp("q1h", [512, T], BF16)
    k1kv = p.inp("k1kv", [128, T], BF16)
    vr1 = p.inp("vr1", [128, 64 * 130], BF16)
    tri = p.inp("tri", [128, 128], BF16)
    identb = p.inp("identb", [128, 128], BF16)
    ot1 = p.out("ot1", [512, T], BF16)

    kbuf = p.sbuf("kbuf", [128, T], BF16)
    vbuf = p.sbuf("vbuf", [128, 64, 130], BF16)
    qbuf = p.sbuf("qbuf", [128, 2, T], BF16)
    pt = p.sbuf("pt", [128, 2, 2, 512], BF16)
    o_sb = p.sbuf("o_sb", [128, 4, 128], BF16)
    rec = p.sbuf("rec", [128, 4], F32)
    lcol = p.sbuf("lcol", [128, 4], F32)
    tri_sb = p.sbuf("tri_sb", [128, 128], BF16)
    ident_sb = p.sbuf("ident_sb1", [128, 128], BF16)
    otst = p.sbuf("otst", [128, 4, 128], BF16)

    ps_s = p.psum("ps_s", [128, 4, 512], F32)
    ps_o = p.psum("ps_o", [128, 4, 512], F32)

    s_in = p.sem("s_in")
    s_q = p.sem("s_q")
    s_pe = p.sem("s_pe")
    s_pv = p.sem("s_pv")
    s_act = p.sem("s_act")
    s_vec = p.sem("s_vec")
    s_st = p.sem("s_st")

    p.dma("gpsimd", tri_sb[:, :], tri, s_in)
    p.dma("gpsimd", ident_sb[:, :], identb, s_in)
    p.dma("sync", kbuf[:, :], k1kv, s_in)
    p.dma("sync", vbuf[:, :, :].rearrange("p a c -> p (a c)"), vr1, s_in)
    consts = s_in.v

    q_guard = {0: 0, 1: 0}
    loads = {}

    def load_q(hl):
        b = hl % 2
        if q_guard[b] > 0:
            p.wait("sync", s_pv, q_guard[b])
        p.dma("sync", qbuf[:, b, :], q1h[hl * 128:(hl + 1) * 128, :], s_q)
        return s_q.v

    loads[0] = load_q(0)
    for hl in range(4):
        b = hl % 2
        if hl + 1 < 4:
            loads[hl + 1] = load_q(hl + 1)
        qv_ = loads[hl]
        evac_act = [0]
        for qg in range(16):
            NP = 2 * qg + 2
            nkc = 4 * qg + 4
            evac_guard = s_vec.v
            exp_hist = []
            mask_hist = []
            pv_hist = []

            def emit_qk(kp):
                pb = kp % 2
                if kp == 0:
                    p.wait("tensor", s_q, qv_)
                    if hl == 0 and qg == 0:
                        p.wait("tensor", s_in, consts)
                    p.wait("tensor", s_vec, evac_guard)
                    p.wait("tensor", s_act, evac_act[0])
                if len(exp_hist) >= 2:
                    p.wait("tensor", s_act, exp_hist[-2])
                for half in range(2):
                    kc = 2 * kp + half
                    p.run("tensor", lambda e, kc=kc, qg=qg, b=b, pb=pb, half=half: e.matmul(
                        ps_s[:, 2 * pb + half, :], kbuf[:, kc * 128:(kc + 1) * 128],
                        qbuf[:, b, qg * 512:(qg + 1) * 512], start=True, stop=True),
                        s_pe if half == 1 else None, 1)
                p.wait("scalar", s_pe, s_pe.v)
                if len(pv_hist) >= 2:
                    p.wait("scalar", s_pv, pv_hist[-2])
                p.run("scalar", lambda e, pb=pb: e.activation(
                    pt[:, pb, :, :], ps_s[:, 2 * pb:2 * pb + 2, :], AF.Exp, scale=SCALE), s_act)
                exp_hist.append(s_act.v)
                nm = 0
                for half in range(2):
                    kc = 2 * kp + half
                    d = kc - 4 * qg
                    if d >= 0:
                        if nm == 0:
                            p.wait("vector", s_act, exp_hist[-1])
                        for j in range(4):
                            if d > j:
                                p.run("vector", lambda e, j=j, pb=pb, half=half: e.memset(
                                    pt[:, pb, half, j * 128:(j + 1) * 128], 0.0), s_vec)
                                nm += 1
                            elif d == j:
                                p.run("vector", lambda e, j=j, pb=pb, half=half: e.tensor_mul(
                                    pt[:, pb, half, j * 128:(j + 1) * 128],
                                    pt[:, pb, half, j * 128:(j + 1) * 128], tri_sb[:, :]), s_vec)
                                nm += 1
                mask_hist.append(s_vec.v if nm else 0)

            def emit_pv(kp):
                pb = kp % 2
                if mask_hist[kp]:
                    p.wait("tensor", s_vec, mask_hist[kp])
                else:
                    p.wait("tensor", s_act, exp_hist[kp])
                for half in range(2):
                    kc = 2 * kp + half
                    for j in range(4):
                        p.run("tensor", lambda e, kc=kc, j=j, pb=pb, half=half, nkc=nkc: e.matmul(
                            ps_o[:, j, 0:129], pt[:, pb, half, j * 128:(j + 1) * 128],
                            vbuf[:, kc, 0:129], start=(kc == 0), stop=(kc == nkc - 1)),
                            s_pv if (half == 1 and j == 3) else None, 1)
                pv_hist.append(s_pv.v)

            emit_qk(0)
            for kp in range(1, NP):
                emit_qk(kp)
                emit_pv(kp - 1)
            emit_pv(NP - 1)

            pv_done = s_pv.v
            for j in range(4):
                p.wait("vector", s_pv, pv_done)
                if j == 0:
                    need = 16 * (p.store_n - 3)
                    if need > 0:
                        p.wait("vector", s_st, need)
                p.run("vector", lambda e, j=j: e.tensor_copy(lcol[:, j:j + 1], ps_o[:, j, 128:129]), s_vec)
                p.wait("vector", s_vec)
                p.run("vector", lambda e, j=j: e.reciprocal(rec[:, j:j + 1], lcol[:, j:j + 1]), s_vec)
                p.wait("vector", s_vec)
                p.run("vector", lambda e, j=j: e.tensor_scalar_mul(
                    o_sb[:, j, :], ps_o[:, j, 0:128], rec[:, j:j + 1]), s_vec)
                p.wait("tensor", s_vec)
                p.run("tensor", lambda e, j=j: e.matmul(ps_s[:, j, 0:128], o_sb[:, j, :], ident_sb[:, :],
                                                        start=True, stop=True), s_pe, 1)
                p.wait("scalar", s_pe)
                p.run("scalar", lambda e, j=j: e.copy(otst[:, j, :], ps_s[:, j, 0:128]), s_act)
                p.wait("sync", s_act)
                p.store_n += 1
                p.dma("sync", ot1[hl * 128:(hl + 1) * 128, qg * 512 + j * 128:qg * 512 + (j + 1) * 128],
                      otst[:, j, :], s_st)
            evac_act[0] = s_act.v
        q_guard[b] = s_pv.v
    p.wait("sync", s_st)
    return p.build()


def run_l1c(d_res):
    qt1f = np.concatenate([d_res[c]["qt1"] for c in range(NCORE)], axis=1)
    kt1f = np.concatenate([d_res[c]["kt1"] for c in range(NCORE)], axis=1)
    v1f = np.concatenate([d_res[c]["v1"] for c in range(NCORE)], axis=0)
    tri = np.tril(np.ones((128, 128), dtype=np.float32)).T.astype(_bf16np())
    nc = _get("l1c", build_l1c)
    in_maps = []
    for c in range(NCORE):
        v4 = v1f[:, c * 128:(c + 1) * 128].reshape(64, 128, 128).transpose(1, 0, 2)
        pad = np.zeros((128, 64, 2), dtype=v4.dtype)
        pad[:, :, 0] = np.float32(1.0)
        vr1 = np.ascontiguousarray(np.concatenate([v4, pad], axis=2).reshape(128, 64 * 130))
        in_maps.append(dict(q1h=np.ascontiguousarray(qt1f[c * 512:(c + 1) * 512]),
                            k1kv=np.ascontiguousarray(kt1f[c * 128:(c + 1) * 128]),
                            vr1=vr1, tri=tri, identb=tri * 0 + np.eye(128, dtype=np.float32).astype(_bf16np())))
    res = run_bass_kernel_spmd(nc, in_maps, core_ids=list(range(NCORE)))
    return res.results



def build_l1d():
    p = Prog("l1d")
    ot1t = p.inp("ot1t", [D, TC], BF16)
    wo1 = p.inp("wo1", [D, D], F32)
    wt1 = p.inp("wt1", [D, D], F32)
    b1 = p.inp("b1", [1, D], F32)
    ocs = p.inp("ocs", [8, D], F32)
    rtt_i = p.inp("rtt", [8, TC], BF16)
    xst_i = p.inp("xst", [D, TC], BF16)
    x1 = p.inp("x1", [TC, D], F32)
    x2 = p.out("x2", [TC, D], F32)

    SW = 128
    NSLAB = D // SW

    otres = p.sbuf("otres1", [128, 32, TC], BF16)
    xsres = p.sbuf("xsres", [128, 32, TC], BF16)
    woslab = p.sbuf("woslab", [128, 2, 32, SW], BF16)
    wtraw = p.sbuf("wtraw", [128, 32, SW], BF16)
    wtsign = p.sbuf("wtsign", [128, 2, 32, SW], BF16)
    rtt_sb = p.sbuf("rtt_sb", [8, TC], BF16)
    ocs_sb = p.sbuf("ocs_sb", [8, D], BF16)
    b1_sb = p.sbuf("b1_sb", [1, D], BF16)
    b1rep = p.sbuf("b1rep", [128, D], BF16)
    ones_k1 = p.sbuf("ones_k1", [1, 512], BF16)
    xp = p.sbuf("xp", [128, 2, SW], F32)
    os_sb = p.sbuf("os_sb", [128, 2, SW], F32)
    rsb = p.sbuf("rsb", [128, SW], F32)
    gsb = p.sbuf("gsb", [128, SW], F32)
    prsb = p.sbuf("prsb", [128, SW], F32)
    x2sb = p.sbuf("x2sb", [128, 2, SW], F32)

    ps_mm = p.psum("ps_mm", [128, 4, 512], F32)
    ps_b = p.psum("ps_b", [128, 2, 512], F32)

    s_in = p.sem("s_in")
    s_x = p.sem("s_x")
    s_w = p.sem("s_w")
    s_ld = p.sem("s_ld")
    s_pe = p.sem("s_pe")
    s_act = p.sem("s_act")
    s_vec = p.sem("s_vec")
    s_st = p.sem("s_st")

    p.dma("gpsimd", ocs_sb[:, :], ocs, s_in)
    p.dma("gpsimd", b1_sb[:, :], b1, s_in)
    p.dma("sync", rtt_sb[:, :], rtt_i, s_in)
    consts = s_in.v
    p.dma("sync", otres[:, :, :], ot1t.rearrange("(c p) n -> p c n", p=128), s_ld)
    p.dma("sync", xsres[:, :, :], xst_i.rearrange("(c p) n -> p c n", p=128), s_ld)
    res_v = s_ld.v
    p.run("vector", lambda e: e.memset(ones_k1[:, :], 1.0), s_vec)
    vconst = s_vec.v

    for g in range(8):
        gb = g % 2
        p.wait("tensor", s_in, consts)
        p.wait("tensor", s_vec, vconst)
        if g >= 2:
            p.wait("tensor", s_vec)
        p.run("tensor", lambda e, g=g, gb=gb: e.matmul(ps_b[:, gb, :], ones_k1[0:1, 0:128], b1_sb[0:1, g * 512:(g + 1) * 512],
                                                       start=True, stop=True), s_pe, 1)
        p.wait("vector", s_pe)
        p.run("vector", lambda e, g=g, gb=gb: e.tensor_copy(b1rep[:, g * 512:(g + 1) * 512], ps_b[:, gb, :]), s_vec)
    b1rep_done = s_vec.v

    mmfree = {}
    bfree = {}
    wslab_guard = {0: 0, 1: 0}
    sraw_guard = [0]
    ld_guard = {0: 0, 1: 0}
    xp_guard = {}

    for sl in range(NSLAB):
        sb = sl % 2
        p.wait("gpsimd", s_pe, wslab_guard[sb])
        p.dma("gpsimd", woslab[:, sb, :, :], wo1[:, sl * SW:(sl + 1) * SW].rearrange("(c p) n -> p c n", p=128), s_w)
        p.wait("gpsimd", s_act, sraw_guard[0])
        p.dma("gpsimd", wtraw[:, :, :], wt1[:, sl * SW:(sl + 1) * SW].rearrange("(c p) n -> p c n", p=128), s_w)
        w_ready = s_w.v
        p.wait("scalar", s_w, w_ready)
        p.wait("scalar", s_pe, wslab_guard[sb])
        p.run("scalar", lambda e, sb=sb: e.activation(wtsign[:, sb, :, :].rearrange("p a b -> p (a b)"),
                                                      wtraw[:, :, :].rearrange("p a b -> p (a b)"), AF.Sign), s_act)
        sraw_guard[0] = s_act.v
        sign_done = s_act.v
        for t in range(NT):
            xb = t % 2
            if xb in xp_guard:
                p.wait("sync", s_vec, xp_guard[xb])
            p.dma("sync", xp[:, xb, :], x1[t * 128:(t + 1) * 128, sl * SW:(sl + 1) * SW], s_x)
            x_ready = s_x.v
            sbank = t % 2
            fbank = 2 + t % 2
            p.wait("tensor", s_w, w_ready)
            if sl == 0 and t == 0:
                p.wait("tensor", s_ld, res_v)
            p.wait("tensor", s_act, sign_done)
            if sbank in mmfree:
                p.wait("tensor", s_vec, mmfree[sbank])
            if fbank in mmfree:
                p.wait("tensor", s_vec, mmfree[fbank])
            for fc in range(32):
                p.run("tensor", lambda e, fc=fc, sb=sb, t=t, sbank=sbank: e.matmul(
                    ps_mm[:, sbank, 0:SW], otres[:, fc, t * 128:(t + 1) * 128], woslab[:, sb, fc, :],
                    start=(fc == 0), stop=(fc == 31)), s_pe if fc == 31 else None, 1)
            s_done = s_pe.v
            for fc in range(32):
                p.run("tensor", lambda e, fc=fc, sb=sb, t=t, fbank=fbank: e.matmul(
                    ps_mm[:, fbank, 0:SW], xsres[:, fc, t * 128:(t + 1) * 128], wtsign[:, sb, fc, :],
                    start=(fc == 0), stop=(fc == 31)), s_pe if fc == 31 else None, 1)
            f_done = s_pe.v
            bbank = t % 2
            if bbank in bfree:
                p.wait("tensor", s_act, bfree[bbank])
            else:
                p.wait("tensor", s_vec, b1rep_done)
            p.run("tensor", lambda e, bbank=bbank, t=t, sl=sl: e.matmul(
                ps_b[:, bbank, 0:SW], rtt_sb[:, t * 128:(t + 1) * 128], ocs_sb[:, sl * SW:(sl + 1) * SW],
                start=True, stop=True), s_pe, 1)
            os_done = s_pe.v
            p.wait("scalar", s_pe, os_done)
            p.run("scalar", lambda e, bbank=bbank: e.copy(os_sb[:, bbank, :], ps_b[:, bbank, 0:SW]), s_act)
            bfree[bbank] = s_act.v
            os_cp = s_act.v
            p.wait("vector", s_act, os_cp)
            p.wait("vector", s_pe, f_done)
            if sl == 0 and t == 0:
                p.wait("vector", s_vec, b1rep_done)
            p.run("vector", lambda e, fbank=fbank, bbank=bbank: e.tensor_mul(rsb[:, :], ps_mm[:, fbank, 0:SW], os_sb[:, bbank, :]), s_vec)
            p.wait("vector", s_vec)
            p.run("vector", lambda e, sl=sl: e.tensor_add(rsb[:, :], rsb[:, :], b1rep[:, sl * SW:(sl + 1) * SW]), s_vec)
            r2_done = s_vec.v
            mmfree[fbank] = s_vec.v
            p.wait("scalar", s_vec, r2_done)
            p.run("scalar", lambda e: e.activation(gsb[:, :], rsb[:, :], AF.Silu), s_act)
            p.wait("vector", s_act)
            p.run("vector", lambda e, sbank=sbank: e.tensor_mul(prsb[:, :], gsb[:, :], ps_mm[:, sbank, 0:SW]), s_vec)
            mmfree[sbank] = s_vec.v
            p.wait("vector", s_vec)
            p.wait("vector", s_x, x_ready)
            need = 16 * (p.store_n - 1) if p.store_n >= 2 else 0
            if need > 0:
                p.wait("vector", s_st, need)
            p.store_n += 1
            p.run("vector", lambda e, xb=xb: e.tensor_add(x2sb[:, xb, :], prsb[:, :], xp[:, xb, :]), s_vec)
            xp_guard[xb] = s_vec.v
            p.wait("sync", s_vec)
            p.dma("sync", x2[t * 128:(t + 1) * 128, sl * SW:(sl + 1) * SW], x2sb[:, xb, :], s_st)
        wslab_guard[sb] = s_pe.v
    p.wait("sync", s_st)
    return p.build()


def run_l1d(inputs, d_res, c1_res):
    ot1T = np.concatenate([c1_res[c]["ot1"] for c in range(NCORE)], axis=0)
    common = dict(
        wo1=np.asarray(inputs["L1_attn_wo"], np.float32),
        wt1=np.ascontiguousarray(np.asarray(inputs["L1_ffn_weight"], np.float32).T),
        b1=np.asarray(inputs["L1_ffn_bias"], np.float32)[None, :],
        ocs=np.asarray(inputs["L1_ffn_out_channel_scale"], np.float32),
    )
    nc = _get("l1d", build_l1d)
    in_maps = []
    for c in range(NCORE):
        in_maps.append(dict(common, ot1t=np.ascontiguousarray(ot1T[:, c * TC:(c + 1) * TC]),
                            rtt=d_res[c]["rtt"], xst=d_res[c]["xst"], x1=d_res[c]["x1"]))
    res = run_bass_kernel_spmd(nc, in_maps, core_ids=list(range(NCORE)))
    return res.results


def kernel(**inputs):
    a = run_l0a(inputs)
    cr = run_l0c(a)
    d = run_l0d_l1a(inputs, a, cr)
    c1 = run_l1c(d)
    e = run_l1d(inputs, d, c1)
    x2 = np.concatenate([e[c]["x2"] for c in range(NCORE)], axis=0)
    return x2.reshape(1, T, D)
